# version 01
# baseline (speedup 1.0000x reference)
import numpy as np
from contextlib import ExitStack

import ml_dtypes
import concourse.bass as bass
import concourse.mybir as mybir
from concourse.bass_utils import run_bass_kernel_spmd

F32 = mybir.dt.float32
BF16 = mybir.dt.bfloat16
AF = mybir.ActivationFunctionType
ALU = mybir.AluOpType

T = 2048
D = 2048
H = 16
DH = 128
CH = 128
NCH = T // CH
KB = D // 128
NT = T // 128
F = 1408
FB = F // 128
NE = 12
FS = 5632
EPS = 1e-5
LNSCALE = float(np.log(np.float32(128.0) ** np.float32(-0.5)))
SQ2048 = float(np.sqrt(np.float32(2048.0)))
SQ128 = float(np.sqrt(np.float32(128.0)))

ENGS = ("t", "v", "s", "g", "y")


class _Inst:
    __slots__ = ("attr", "args", "kw", "incs")

    def __init__(self, attr, args, kw):
        self.attr, self.args, self.kw = attr, args, kw
        self.incs = []

    def then_inc(self, sem, v=1):
        self.incs.append((sem, v))
        return self


class Rec:

    SELF = ("v", "s")

    def __init__(self):
        self.ev = {e: [] for e in ENGS}
        self.cnt = {e: 0 for e in self.SELF}
        self.selfsem = {}

    def op(self, eng, attr, *args, **kw):
        inst = _Inst(attr, args, kw)
        self.ev[eng].append(inst)
        if eng in self.SELF and attr != "wait_ge":
            self.cnt[eng] += 1
        return inst

    def mark(self, eng, sem):
        if eng in self.SELF:
            return self.cnt[eng]
        for inst in reversed(self.ev[eng]):
            if inst.attr != "wait_ge":
                for (h, _v) in inst.incs:
                    if h is sem.h:
                        return sem.n
                sem.bump()
                inst.incs.append((sem.h, 1))
                return sem.n
        raise AssertionError(f"no real instruction to mark on {eng}")

    def replay(self, eng_name, eng_obj):
        if eng_name in self.SELF:
            sem = self.selfsem[eng_name]
            idx = 0
            for inst in self.ev[eng_name]:
                if inst.attr == "wait_ge":
                    getattr(eng_obj, inst.attr)(*inst.args, **inst.kw)
                    continue
                if idx > 0:
                    eng_obj.wait_ge(sem, idx)
                idx += 1
                real = getattr(eng_obj, inst.attr)(*inst.args, **inst.kw)
                real.then_inc(sem, 1)
                for (s, v) in inst.incs:
                    real.then_inc(s, v)
            return
        for inst in self.ev[eng_name]:
            real = getattr(eng_obj, inst.attr)(*inst.args, **inst.kw)
            for (sem, v) in inst.incs:
                real.then_inc(sem, v)


class Sem:
    def __init__(self, es, nc, name, rec=None, eng=None):
        self.h = es.enter_context(nc.semaphore(name))
        self._n = 0
        self._rec, self._eng = rec, eng

    @property
    def n(self):
        if self._rec is not None:
            return self._rec.cnt[self._eng]
        return self._n

    def bump(self, k=1):
        assert self._rec is None
        self._n += k
        return self.h


def build_nc(phase_limit="full"):
    nc = bass.Bass("TRN2", target_bir_lowering=False)

    class _Done(Exception):
        pass

    es = ExitStack()
    R = Rec()

    def din(name, shape, dt=F32):
        return nc.dram_tensor(name, list(shape), dt, kind="ExternalInput").ap()

    xseg = din("xseg", (T, D))
    xpre = din("xpre", (T, D))
    zcol_h = din("zcol", (128, 1))
    wq_h = din("wq", (D, D)); wk_h = din("wk", (D, D)); wv_h = din("wv", (D, D))
    wgk1_h = din("wgk1", (D, 16)); wgk2_h = din("wgk2", (16, D)); bgk2_h = din("bgk2", (D,))
    wgate_h = din("wgate", (D, D)); gnw_h = din("gnw", (DH,)); wo_h = din("wo", (D, D))
    wr_h = din("wr", (D, 8)); wsg_h = din("wsg", (D, 1))
    ewg_h = din("ewg", (8, D, F)); ewu_h = din("ewu", (8, D, F)); ewd_h = din("ewd", (8, F, D))
    sgw_h = din("sgw", (D, FS)); suw_h = din("suw", (D, FS)); sdw_h = din("sdw", (FS, D))

    y_out = nc.dram_tensor("y", [T, D], F32, kind="ExternalOutput").ap()
    h_hbm = nc.dram_tensor("h_scr", [NE * F, T], BF16, kind="Internal").ap()
    out1_hbm = nc.dram_tensor("out1_scr", [T, D], F32, kind="Internal").ap()
    og_hbm = nc.dram_tensor("og_scr", [H * DH, T], BF16, kind="Internal").ap()

    eye_f_c = nc.inline_tensor(np.eye(128, dtype=np.float32), "eye_f").ap()
    eye_b_c = nc.inline_tensor(np.eye(128, dtype=ml_dtypes.bfloat16), "eye_b").ap()
    triu_c = nc.inline_tensor(np.triu(np.ones((128, 128), np.float32)), "triu").ap()
    zeros_c = nc.inline_tensor(np.zeros((128, 512), np.float32), "zeros512").ap()
    rk0 = np.zeros((128, NT * 9), np.float32)
    for j in range(NT):
        for e in range(8):
            rk0[:, 9 * j + e] = e
    rank0_c = nc.inline_tensor(rk0, "rank0").ap()

    sT = Sem(es, nc, "sT")
    sV = Sem(es, nc, "sV", rec=R, eng="v")
    sS = Sem(es, nc, "sS", rec=R, eng="s")
    cC = Sem(es, nc, "cC"); cRK = Sem(es, nc, "cRK")
    cX = [Sem(es, nc, "cX0"), Sem(es, nc, "cX1")]
    cO1 = [Sem(es, nc, "cO1a"), Sem(es, nc, "cO1b")]
    cOG = [Sem(es, nc, "cOGa"), Sem(es, nc, "cOGb")]
    cOS = [Sem(es, nc, "cOSa"), Sem(es, nc, "cOSb")]
    cXW = [Sem(es, nc, "cXWa"), Sem(es, nc, "cXWb")]
    cH = [Sem(es, nc, "cHa"), Sem(es, nc, "cHb")]
    cHL = [Sem(es, nc, "cHLa"), Sem(es, nc, "cHLb")]
    cACC = Sem(es, nc, "cACC"); cY = Sem(es, nc, "cY")
    cRT = Sem(es, nc, "cRT")
    cWR = [Sem(es, nc, f"cWR{i}") for i in range(5)]
    cGK = Sem(es, nc, "cGK")
    cGU = [Sem(es, nc, f"cGU{i}") for i in range(4)]
    cWD = [Sem(es, nc, "cWD0"), Sem(es, nc, "cWD1")]
    cWO = Sem(es, nc, "cWO")
    DMASEMS = ([cC, cRK, cACC, cY, cRT, cGK, cWO] + cX + cO1 + cOG + cOS
               + cXW + cH + cHL + cWR + cGU + cWD)

    def sbl(name, shape, dt):
        return es.enter_context(nc.sbuf_tensor(name, list(shape), dt, side="left"))

    eye_fs = sbl("eye_fs", (128, 128), F32)
    eye_bs = sbl("eye_bs", (128, 128), BF16)
    triu_s = sbl("triu_s", (128, 128), F32)
    zeros_s = sbl("zeros_s", (128, 512), F32)
    cols = sbl("cols", (128, 8), F32)
    gnorm_c = sbl("gnorm_c", (128, 1), F32)
    C_nat = sbl("C_nat", (128, NT * 8), F32)
    sgate_n = sbl("sgate_n", (128, NT), F32)
    S_in = sbl("S_in", (128, H * DH), F32)
    z1b = sbl("z1b", (32, T), BF16)
    wgk1_b = sbl("wgk1_b", (128, KB * 16), BF16)
    wgk2_b = sbl("wgk2_b", (32, DH), BF16)
    wr9_hi = sbl("wr9_hi", (128, KB * 9), BF16)
    wr9_lo = sbl("wr9_lo", (128, KB * 9), BF16)
    stat = sbl("stat", (128, 8), F32)
    xnT_es = ExitStack()
    xnT = xnT_es.enter_context(nc.sbuf_tensor("xnT", [128, KB * T], BF16, side="left"))

    CZ, CE2048, CE128, CONE = 0, 1, 2, 3

    R.op("y", "dma_start", eye_fs[:, :], eye_f_c).then_inc(cC.bump(16), 16)
    R.op("y", "dma_start", eye_bs[:, :], eye_b_c).then_inc(cC.bump(16), 16)
    R.op("y", "dma_start", triu_s[:, :], triu_c).then_inc(cC.bump(16), 16)
    R.op("y", "dma_start", zeros_s[:, :], zeros_c).then_inc(cC.bump(16), 16)
    R.op("y", "dma_start", cols[:, CZ:CZ + 1], zcol_h).then_inc(cC.bump(16), 16)
    R.op("y", "dma_start", gnorm_c[:, 0:1],
         gnw_h.rearrange("(p o) -> p o", o=1)).then_inc(cC.bump(16), 16)
    consts_ld = cC.n
    R.op("v", "memset", cols[:, CE2048:CE2048 + 1], 2048.0 * EPS)
    R.op("v", "memset", cols[:, CE128:CE128 + 1], 128.0 * EPS)
    R.op("v", "memset", cols[:, CONE:CONE + 1], 1.0)
    R.op("v", "wait_ge", cC.h, consts_ld)
    R.op("v", "tensor_scalar_mul", gnorm_c[:, 0:1], gnorm_c[:, 0:1], SQ128)
    R.mark("v", sV)

    rs0 = ExitStack()
    wr9_f = rs0.enter_context(nc.sbuf_tensor("wr9_f", [128, KB * 9], F32, side="right"))
    wrf = rs0.enter_context(nc.sbuf_tensor("wrf", [128, KB * 8], F32, side="right"))
    wsgT = rs0.enter_context(nc.sbuf_tensor("wsgT", [16, 128], F32, side="right"))
    R.op("g", "dma_start", wrf[:, :].rearrange("p (kb e) -> p kb e", kb=KB),
         wr_h.rearrange("(kb p) e -> p kb e", p=128)).then_inc(cRT.bump(16), 16)
    R.op("g", "dma_start", wsgT[:, :],
         wsg_h.rearrange("(kb p) o -> kb (p o)", p=128)).then_inc(cRT.bump(16), 16)
    R.op("g", "dma_start", wgk1_b[:, :].rearrange("p (kb e) -> p kb e", kb=KB),
         wgk1_h.rearrange("(kb p) e -> p kb e", p=128)).then_inc(cRT.bump(16), 16)
    wr9_ld = cRT.n

    phase_done_cb = [None]

    def phase_done(name):
        if phase_limit == name:
            raise _Done()
    phase_done_cb[0] = phase_done

    def barrier():
        R.mark("t", sT)
        R.mark("v", sV)
        R.mark("s", sS)
        waits = [(sT.h, sT.n), (sV.h, sV.n), (sS.h, sS.n)] + [
            (d.h, d.n) for d in DMASEMS if d.n > 0]
        for e in ENGS:
            for (hh, vv) in waits:
                if vv > 0:
                    R.op(e, "wait_ge", hh, vv)

    rs1 = ExitStack()

    def sbr1(name, shape, dt):
        return rs1.enter_context(nc.sbuf_tensor(name, list(shape), dt, side="right"))

    xt = [sbr1("xt0", (128, D), F32), sbr1("xt1", (128, D), F32)]
    xnf = sbr1("xnf", (128, D), F32)
    sqtrash = sbr1("sqtrash", (128, D), F32)
    loT = sbr1("loT", (128, 512), BF16)
    wring = [sbr1(f"wring{i}", (128, KB * 128), BF16) for i in range(5)]
    gku = sbr1("gku", (128, T), F32)
    Gg = sbr1("Gg", (128, T), F32)
    tmpA = sbr1("tmpA", (128, 512), F32)
    tmpB = sbr1("tmpB", (128, 512), F32)
    tmpC = sbr1("tmpC", (128, 512), F32)
    kT = sbr1("kT", (128, T), BF16)
    qT = sbr1("qT", (128, T), BF16)
    vnat = sbr1("vnat", (128, T), BF16)
    tmpv = sbr1("tmpv", (128, 512), BF16)
    ccol = sbr1("ccol", (128, 16 * 8), F32)
    CEND, CPRV, CB1, CB2, CC1, CC2, CDEC, CTMP = range(8)
    kwbf = sbr1("kwbf", (128, 128), BF16)
    kwn = sbr1("kwn", (128, 128), BF16)
    e1buf = sbr1("e1buf", (128, 128), F32)
    e2buf = sbr1("e2buf", (128, 128), F32)
    Sf = sbr1("Sf", (128, 128), F32)
    Sbf = sbr1("Sbf", (128, 128), BF16)
    sgT = sbr1("sgT", (128, T), BF16)
    qgbf = sbr1("qgbf", (128, 128), BF16)
    qhbf = sbr1("qhbf", (128, 128), BF16)
    khbf = sbr1("khbf", (128, 128), BF16)
    atbf = sbr1("atbf", (128, 128), BF16)
    of32 = sbr1("of32", (128, 128), F32)
    osc = sbr1("osc", (128, 128), BF16)
    tr128 = sbr1("tr128", (128, 128), BF16)
    ogh = [sbr1("ogh0", (128, T), BF16), sbr1("ogh1", (128, T), BF16)]

    G = {
        "proj_v": [0, 0], "proj_s": [0, 0], "vt": 0, "vt_t": 0, "scf": 0, "scb": 0,
        "wring_tok": [0] * 5, "wslot": 0, "wgk2_tok": 0, "wgk2_ld": 0,
        "pstr_s": [0, 0], "pstr_v": [0, 0], "pstr_t": [0, 0],
        "xnf_t": 0, "lo_t": 0, "head_s_guard": 0, "chunk_pe": 0,
        "ebuf_v": 0, "pkw_pe": 0, "pkw_v": 0,
    }
    PS = {}

    def ln_phase(x_src, is_seg):
        first_bank = [True, True, True, True]
        for i in range(NT):
            buf = i % 2
            if G[f"xt_guard{buf}"]:
                R.op("y", "wait_ge", sV.h, G[f"xt_guard{buf}"])
            R.op("y", "dma_start", xt[buf][:, :],
                 x_src[i * 128:(i + 1) * 128, :]).then_inc(cX[buf].bump(16), 16)
            x_ld = cX[buf].n
            R.op("v", "wait_ge", cX[buf].h, x_ld)
            R.op("v", "tensor_scalar", sqtrash[:, :], xt[buf][:, :], 1.0, 0.0, ALU.mult,
                 ALU.add, accum_out=stat[:, 0:1])
            R.op("v", "tensor_scalar_mul", stat[:, 1:2], stat[:, 0:1], -1.0 / 2048.0)
            R.mark("v", sV)
            v_nm = sV.n
            R.op("s", "wait_ge", sV.h, v_nm)
            R.op("s", "activation", sqtrash[:, :], xt[buf][:, :], AF.Square,
                 bias=stat[:, 1:2], accum_out=stat[:, 2:3])
            R.op("s", "activation", stat[:, 3:4], stat[:, 2:3], AF.Ln,
                 bias=cols[:, CE2048:CE2048 + 1])
            R.op("s", "activation", stat[:, 4:5], stat[:, 3:4], AF.Exp, scale=-0.5)
            R.mark("s", sS)
            s_r = sS.n
            R.op("v", "wait_ge", sS.h, s_r)
            R.op("v", "tensor_scalar_mul", stat[:, 5:6], stat[:, 4:5], SQ2048)
            if G["xnf_t"]:
                R.op("v", "wait_ge", sT.h, G["xnf_t"])
            R.op("v", "tensor_scalar", xnf[:, :], xt[buf][:, :], stat[:, 1:2],
                 stat[:, 5:6], ALU.add, ALU.mult)
            R.mark("v", sV)
            v_xn = sV.n
            G[f"xt_guard{buf}"] = v_xn
            for grp in range(4):
                pb = grp % 2
                R.op("t", "wait_ge", sV.h, v_xn)
                if G["pstr_s"][pb]:
                    R.op("t", "wait_ge", sS.h, G["pstr_s"][pb])
                if G["pstr_v"][pb]:
                    R.op("t", "wait_ge", sV.h, G["pstr_v"][pb])
                for q in range(4):
                    kb = grp * 4 + q
                    R.op("t", "transpose", PS["tr"][pb][:, q * 128:(q + 1) * 128],
                         xnf[:, kb * 128:(kb + 1) * 128], eye_fs[:, :])
                R.mark("t", sT)
                t_tr = sT.n
                if grp == 3:
                    G["xnf_t"] = t_tr
                R.op("s", "wait_ge", sT.h, t_tr)
                for q in range(4):
                    kb = grp * 4 + q
                    R.op("s", "copy", xnT[:, kb * T + i * 128: kb * T + (i + 1) * 128],
                         PS["tr"][pb][:, q * 128:(q + 1) * 128])
                R.mark("s", sS)
                G["pstr_s"][pb] = sS.n
                if is_seg:
                    R.op("v", "wait_ge", sS.h, sS.n)
                    if G["lo_t"]:
                        R.op("v", "wait_ge", sT.h, G["lo_t"])
                    for q in range(4):
                        kb = grp * 4 + q
                        R.op("v", "tensor_tensor", loT[:, q * 128:(q + 1) * 128],
                             PS["tr"][pb][:, q * 128:(q + 1) * 128],
                             xnT[:, kb * T + i * 128: kb * T + (i + 1) * 128], ALU.subtract)
                    R.mark("v", sV)
                    v_lo = sV.n
                    G["pstr_v"][pb] = v_lo
                    bank_i = i // 4
                    R.op("t", "wait_ge", sV.h, v_lo)
                    for q in range(4):
                        kb = grp * 4 + q
                        st = first_bank[bank_i]
                        first_bank[bank_i] = False
                        hi_ap = xnT[:, kb * T + i * 128: kb * T + (i + 1) * 128]
                        last = (i % 4 == 3) and (kb == KB - 1)
                        R.op("t", "matmul", PS["lg"][0:9, i * 128:(i + 1) * 128],
                             wr9_hi[:, kb * 9:(kb + 1) * 9], hi_ap, start=st, stop=False)
                        R.op("t", "matmul", PS["lg"][0:9, i * 128:(i + 1) * 128],
                             wr9_lo[:, kb * 9:(kb + 1) * 9], hi_ap, start=False, stop=False)
                        R.op("t", "matmul", PS["lg"][0:9, i * 128:(i + 1) * 128],
                             wr9_hi[:, kb * 9:(kb + 1) * 9], loT[:, q * 128:(q + 1) * 128],
                             start=False, stop=last)
                    R.mark("t", sT)
                    G["pstr_t"][pb] = sT.n
                    G["lo_t"] = sT.n

    G["xt_guard0"] = G["xt_guard1"] = 0

    def compute_z1():
        R.op("v", "memset", z1b[0:32, :], 1.0)
        R.mark("v", sV)
        v_ones = sV.n
        R.op("t", "wait_ge", cRT.h, wr9_ld)
        for kb in range(KB):
            for n in range(4):
                R.op("t", "matmul", PS["lg"][0:16, n * 512:(n + 1) * 512],
                     wgk1_b[:, kb * 16:(kb + 1) * 16],
                     xnT[:, kb * T + n * 512: kb * T + (n + 1) * 512],
                     start=(kb == 0), stop=(kb == KB - 1))
        R.mark("t", sT)
        t_z1 = sT.n
        R.op("v", "wait_ge", sT.h, t_z1)
        R.op("v", "wait_ge", sV.h, v_ones)
        R.op("v", "tensor_copy", z1b[0:16, :], PS["lg"][0:16, :])
        R.mark("v", sV)
        return sV.n

    def load_whead(col_src):
        sl = G["wslot"] % 5
        G["wslot"] += 1
        if G["wring_tok"][sl]:
            R.op("g", "wait_ge", sT.h, G["wring_tok"][sl])
        R.op("g", "dma_start", wring[sl][:, :].rearrange("p (kb m) -> p kb m", kb=KB),
             col_src).then_inc(cWR[sl].bump(16), 16)
        return sl, cWR[sl].n

    def load_wgk2(h):
        if G["wgk2_tok"]:
            R.op("g", "wait_ge", sT.h, G["wgk2_tok"])
        R.op("g", "dma_start", wgk2_b[0:16, :],
             wgk2_h[:, h * DH:(h + 1) * DH]).then_inc(cGK.bump(16), 16)
        R.op("g", "dma_start", wgk2_b[16:17, :],
             bgk2_h.rearrange("(o d) -> o d", o=1)[:, h * DH:(h + 1) * DH]).then_inc(cGK.bump(16), 16)
        G["wgk2_ld"] = cGK.n

    def proj_wait(pb):
        if G["proj_v"][pb]:
            R.op("t", "wait_ge", sV.h, G["proj_v"][pb])
        if G["proj_s"][pb]:
            R.op("t", "wait_ge", sS.h, G["proj_s"][pb])

    def mm_proj(slot, w_ld, out_T, mask=False):
        for half in range(2):
            pb = half
            R.op("t", "wait_ge", cWR[slot].h, w_ld)
            proj_wait(pb)
            for kb in range(KB):
                for n in range(2):
                    off = kb * T + half * 1024 + n * 512
                    R.op("t", "matmul", PS["proj"][pb][:, n * 512:(n + 1) * 512],
                         wring[slot][:, kb * 128:(kb + 1) * 128],
                         xnT[:, off:off + 512], start=(kb == 0), stop=(kb == KB - 1))
            R.mark("t", sT)
            t_mm = sT.n
            G["wring_tok"][slot] = t_mm
            R.op("v", "wait_ge", sT.h, t_mm)
            if mask:
                R.op("v", "tensor_scalar_mul", out_T[:, half * 1024:(half + 1) * 1024],
                     PS["proj"][pb][:, :], cols[:, CZ:CZ + 1])
            else:
                R.op("v", "tensor_copy", out_T[:, half * 1024:(half + 1) * 1024],
                     PS["proj"][pb][:, :])
            R.mark("v", sV)
            G["proj_v"][pb] = sV.n

    def mm_proj_v(slot, w_ld):
        for half in range(2):
            pb = half
            R.op("t", "wait_ge", cWR[slot].h, w_ld)
            proj_wait(pb)
            for kb in range(KB):
                for n in range(2):
                    off = kb * T + half * 1024 + n * 512
                    R.op("t", "matmul", PS["proj"][pb][:, n * 512:(n + 1) * 512],
                         wring[slot][:, kb * 128:(kb + 1) * 128],
                         xnT[:, off:off + 512], start=(kb == 0), stop=(kb == KB - 1))
            R.mark("t", sT)
            t_mm = sT.n
            G["wring_tok"][slot] = t_mm
            for n in range(2):
                R.op("v", "wait_ge", sT.h, t_mm)
                if G["vt_t"]:
                    R.op("v", "wait_ge", sT.h, G["vt_t"])
                R.op("v", "tensor_copy", tmpv[:, :], PS["proj"][pb][:, n * 512:(n + 1) * 512])
                R.mark("v", sV)
                v_tm = sV.n
                G["proj_v"][pb] = v_tm
                R.op("t", "wait_ge", sV.h, v_tm)
                if G["vt"]:
                    R.op("t", "wait_ge", sV.h, G["vt"])
                for q in range(4):
                    R.op("t", "transpose", PS["vt"][:, q * 128:(q + 1) * 128],
                         tmpv[:, q * 128:(q + 1) * 128], eye_bs[:, :])
                R.mark("t", sT)
                G["vt_t"] = sT.n
                R.op("v", "wait_ge", sT.h, sT.n)
                base = half * 1024 + n * 512
                R.op("v", "tensor_copy", vnat[:, base:base + 512], PS["vt"][:, :])
                R.mark("v", sV)
                G["vt"] = sV.n

    def mm_proj_gate(slot, w_ld):
        for half in range(2):
            pb = half
            R.op("t", "wait_ge", cWR[slot].h, w_ld)
            proj_wait(pb)
            for kb in range(KB):
                for n in range(2):
                    off = kb * T + half * 1024 + n * 512
                    R.op("t", "matmul", PS["proj"][pb][:, n * 512:(n + 1) * 512],
                         wring[slot][:, kb * 128:(kb + 1) * 128],
                         xnT[:, off:off + 512], start=(kb == 0), stop=(kb == KB - 1))
            R.mark("t", sT)
            t_mm = sT.n
            G["wring_tok"][slot] = t_mm
            R.op("s", "wait_ge", sT.h, t_mm)
            if G.get("sgT_guard"):
                R.op("s", "wait_ge", sV.h, G["sgT_guard"])
            R.op("s", "activation", sgT[:, half * 1024:(half + 1) * 1024],
                 PS["proj"][pb][:, :], AF.Silu)
            R.mark("s", sS)
            G["proj_s"][pb] = sS.n

    def mm_z2_gk(v_z1):
        for half in range(2):
            pb = half
            proj_wait(pb)
            R.op("t", "wait_ge", sV.h, v_z1)
            R.op("t", "wait_ge", cGK.h, G["wgk2_ld"])
            for n in range(2):
                off = half * 1024 + n * 512
                R.op("t", "matmul", PS["proj"][pb][:, n * 512:(n + 1) * 512],
                     wgk2_b[0:17, :], z1b[0:17, off:off + 512], start=True, stop=True)
            R.mark("t", sT)
            t_z2 = sT.n
            G["wgk2_tok"] = t_z2
            phase_done_cb[0]("q0" if half == 0 else "q0b")
            for n in range(2):
                qoff = half * 1024 + n * 512
                src = PS["proj"][pb][:, n * 512:(n + 1) * 512]
                R.op("s", "wait_ge", sT.h, t_z2)
                if n or half:
                    R.op("s", "wait_ge", sV.h, sV.n)
                R.op("s", "activation", tmpA[:, :], src, AF.Abs)
                s_abs = R.mark("s", sS)
                R.op("s", "activation", tmpB[:, :], tmpA[:, :], AF.Exp, scale=-1.0)
                R.op("s", "activation", tmpA[:, :], tmpB[:, :], AF.Ln,
                     bias=cols[:, CONE:CONE + 1])
                s_ln = R.mark("s", sS)
                R.op("v", "wait_ge", sS.h, s_abs)
                R.op("v", "tensor_scalar_min", tmpC[:, :], src, 0.0)
                R.op("v", "wait_ge", sS.h, s_ln)
                R.op("v", "tensor_tensor", gku[:, qoff:qoff + 512], tmpC[:, :],
                     tmpA[:, :], ALU.subtract)
                R.mark("v", sV)
                phase_done_cb[0](f"q{half}{n}")
            G["proj_v"][pb] = sV.n
            G["proj_s"][pb] = sS.n
        phase_done_cb[0]("g1")
        if G["head_s_guard"]:
            R.op("v", "wait_ge", sS.h, G["head_s_guard"])
        for s4 in range(4):
            init = 0.0 if s4 == 0 else Gg[:, s4 * 512 - 1: s4 * 512]
            R.op("v", "tensor_tensor_scan", Gg[:, s4 * 512:(s4 + 1) * 512],
                 gku[:, s4 * 512:(s4 + 1) * 512], zeros_s[:, :], init, ALU.add, ALU.add)
        phase_done_cb[0]("g2")
        R.op("v", "tensor_copy", ccol[:, CEND * 16:CEND * 16 + 16], Gg[:, 127::128])
        R.op("v", "memset", ccol[:, CPRV * 16:CPRV * 16 + 1], 0.0)
        R.op("v", "tensor_copy", ccol[:, CPRV * 16 + 1:CPRV * 16 + 16],
             Gg[:, 127:T - 128:128])
        R.mark("v", sV)
        phase_done_cb[0]("g3")
        return sV.n

    try:
        psA = ExitStack()
        PS["tr"] = [psA.enter_context(nc.psum_tensor("ps_tr0", [128, 512], F32)),
                    psA.enter_context(nc.psum_tensor("ps_tr1", [128, 512], F32))]
        PS["lg"] = psA.enter_context(nc.psum_tensor("ps_lg", [16, 2048], F32))
        R.op("t", "wait_ge", cRT.h, wr9_ld)
        R.op("t", "wait_ge", cC.h, consts_ld)
        R.op("t", "transpose", PS["tr"][0][:, 0:16], wsgT[:, :], eye_fs[0:16, 0:16])
        t_wsg = R.mark("t", sT)
        R.op("v", "wait_ge", sT.h, t_wsg)
        R.op("v", "wait_ge", cRT.h, wr9_ld)
        R.op("v", "tensor_copy",
             wr9_f[:, :].rearrange("p (kb e) -> p kb e", kb=KB)[:, :, 0:8],
             wrf[:, :].rearrange("p (kb e) -> p kb e", kb=KB))
        R.op("v", "tensor_copy",
             wr9_f[:, :].rearrange("p (kb e) -> p kb e", kb=KB)[:, :, 8:9],
             PS["tr"][0][:, 0:16].rearrange("p (kb o) -> p kb o", o=1))
        R.op("v", "tensor_copy", wr9_hi[:, :], wr9_f[:, :])
        R.op("v", "tensor_tensor", wr9_lo[:, :], wr9_f[:, :], wr9_hi[:, :], ALU.subtract)
        v_wr9 = R.mark("v", sV)
        G["pstr_v"][0] = v_wr9
        ln_phase(xpre, False)
        barrier()
        v_z1 = compute_z1()
        barrier()
        phase_done("lnpre")
        psA.close()

        psB = ExitStack()
        PS["proj"] = [psB.enter_context(nc.psum_tensor("ps_pA", [128, 1024], F32)),
                      psB.enter_context(nc.psum_tensor("ps_pB", [128, 1024], F32))]
        PS["vt"] = psB.enter_context(nc.psum_tensor("ps_vt", [128, 512], BF16))
        PS["scf"] = psB.enter_context(nc.psum_tensor("ps_scf", [128, 512], F32))
        PS["scb"] = psB.enter_context(nc.psum_tensor("ps_scb", [128, 1024], BF16))

        for h in range(H):
            slk, ldk = load_whead(wk_h.rearrange("(kb p) m -> p kb m", p=128)[:, :, h * DH:(h + 1) * DH])
            slv, ldv = load_whead(wv_h.rearrange("(kb p) m -> p kb m", p=128)[:, :, h * DH:(h + 1) * DH])
            load_wgk2(h)
            mm_proj(slk, ldk, kT, mask=True)
            if h == 0:
                barrier()
                phase_done("p_projk")
            mm_proj_v(slv, ldv)
            if h == 0:
                barrier()
                phase_done("p_projv")
            v_gk = mm_z2_gk(v_z1)
            if h == 0:
                barrier()
                phase_done("p_gk")
            R.op("v", "tensor_scalar_mul", ccol[:, CB2 * 16:CB2 * 16 + 16],
                 ccol[:, CEND * 16:CEND * 16 + 16], 1.0 / 16.0)
            R.op("v", "tensor_tensor", ccol[:, CTMP * 16:CTMP * 16 + 16],
                 ccol[:, CEND * 16:CEND * 16 + 16], ccol[:, CPRV * 16:CPRV * 16 + 16],
                 ALU.subtract)
            R.op("v", "memset", Sf[:, :], 0.0)
            R.mark("v", sV)
            v_cols = sV.n
            R.op("s", "wait_ge", sV.h, v_cols)
            R.op("s", "activation", ccol[:, CDEC * 16:CDEC * 16 + 16],
                 ccol[:, CTMP * 16:CTMP * 16 + 16], AF.Exp, scale=1.0 / 16.0)
            R.mark("s", sS)
            s_dec = sS.n
            for j in range(NCH):
                R.op("s", "wait_ge", sV.h, v_cols)
                if G["pkw_v"]:
                    R.op("s", "wait_ge", sV.h, G["pkw_v"])
                R.op("s", "activation", e2buf[:, :], Gg[:, j * 128:(j + 1) * 128], AF.Exp,
                     scale=-1.0 / 16.0, bias=ccol[:, CB2 * 16 + j:CB2 * 16 + j + 1])
                R.mark("s", sS)
                s_e2 = sS.n
                R.op("v", "wait_ge", sS.h, s_e2)
                if G["pkw_pe"]:
                    R.op("v", "wait_ge", sT.h, G["pkw_pe"])
                R.op("v", "tensor_tensor", kwbf[:, :], kT[:, j * 128:(j + 1) * 128],
                     e2buf[:, :], ALU.mult)
                R.mark("v", sV)
                v_kw = sV.n
                R.op("t", "wait_ge", sV.h, v_kw)
                if G["scb"]:
                    R.op("t", "wait_ge", sV.h, G["scb"])
                R.op("t", "transpose", PS["scb"][:, 0:128], kwbf[:, :], eye_bs[:, :])
                R.mark("t", sT)
                t_kwt = sT.n
                G["pkw_pe"] = t_kwt
                R.op("v", "wait_ge", sT.h, t_kwt)
                R.op("v", "tensor_copy", kwn[:, :], PS["scb"][:, 0:128])
                R.mark("v", sV)
                v_kwn = sV.n
                G["scb"] = v_kwn
                G["pkw_v"] = v_kwn
                R.op("t", "wait_ge", sV.h, v_kwn)
                if G["scf"]:
                    R.op("t", "wait_ge", sV.h, G["scf"])
                R.op("t", "matmul", PS["scf"][:, 256:384], kwn[:, :],
                     vnat[:, j * 128:(j + 1) * 128], start=True, stop=True)
                R.mark("t", sT)
                t_su = sT.n
                R.op("v", "wait_ge", sT.h, t_su)
                R.op("v", "wait_ge", sS.h, s_dec)
                R.op("v", "scalar_tensor_tensor", Sf[:, :], Sf[:, :],
                     ccol[:, CDEC * 16 + j:CDEC * 16 + j + 1], PS["scf"][:, 256:384],
                     ALU.mult, ALU.add)
                R.mark("v", sV)
                G["scf"] = sV.n
            R.op("v", "tensor_copy", S_in[:, h * DH:(h + 1) * DH], Sf[:, :])
            R.mark("v", sV)
            G["head_s_guard"] = sS.n
            if h == 0:
                barrier()
                phase_done("p_head0")

        barrier()
        phase_done("prefix")
        psB.close()

        psA2 = ExitStack()
        PS["tr"] = [psA2.enter_context(nc.psum_tensor("ps_tr0b", [128, 512], F32)),
                    psA2.enter_context(nc.psum_tensor("ps_tr1b", [128, 512], F32))]
        PS["lg"] = psA2.enter_context(nc.psum_tensor("ps_lgb", [16, 2048], F32))
        G["pstr_s"] = [0, 0]
        G["pstr_v"] = [0, 0]
        ln_phase(xseg, True)
        barrier()
        phase_done("lnseg")

        rsr = ExitStack()

        def sbrr(name, shape, dt):
            return rsr.enter_context(nc.sbuf_tensor(name, list(shape), dt, side="right"))

        lgT = sbrr("lgT", (16, NT * 128), F32)
        lg_n = sbrr("lg_n", (128, NT * 9), F32)
        es_n = sbrr("es_n", (128, NT * 9), F32)
        rank_n = sbrr("rank_n", (128, NT * 9), F32)
        rk_init = sbrr("rk_init", (128, NT * 9), F32)
        s1_n = sbrr("s1_n", (128, NT), F32)
        m1v = sbrr("m1v", (128, NT), F32)

        R.op("y", "dma_start", rk_init[:, :], rank0_c).then_inc(cRK.bump(16), 16)
        rk_ld = cRK.n
        R.op("v", "memset", lgT[0:16, :], 0.0)
        R.op("v", "tensor_copy", lgT[0:9, :], PS["lg"][0:9, :])
        R.mark("v", sV)
        v_lgT = sV.n
        for i in range(NT):
            R.op("t", "wait_ge", sV.h, v_lgT)
            if i:
                R.op("t", "wait_ge", sV.h, sV.n)
            R.op("t", "transpose", PS["tr"][0][:, 0:16], lgT[0:16, i * 128:(i + 1) * 128],
                 eye_fs[0:16, 0:16])
            R.mark("t", sT)
            R.op("v", "wait_ge", sT.h, sT.n)
            R.op("v", "tensor_copy", lg_n[:, i * 9:(i + 1) * 9], PS["tr"][0][:, 0:9])
            R.mark("v", sV)
        R.op("v", "tensor_copy", m1v[:, :], lg_n[:, 0::9])
        for e in range(1, 8):
            R.op("v", "tensor_tensor", m1v[:, :], m1v[:, :], lg_n[:, e::9], ALU.max)
        for i in range(NT):
            R.op("v", "tensor_scalar_sub", es_n[:, i * 9:i * 9 + 9],
                 lg_n[:, i * 9:i * 9 + 9], m1v[:, i:i + 1])
        R.mark("v", sV)
        v_sub = sV.n
        R.op("s", "wait_ge", sV.h, v_sub)
        R.op("s", "activation", es_n[:, :], es_n[:, :], AF.Exp)
        R.op("s", "activation", sgate_n[:, :], lg_n[:, 8::9], AF.Exp, scale=-1.0)
        R.mark("s", sS)
        s_es = sS.n
        R.op("v", "wait_ge", sS.h, s_es)
        R.op("v", "tensor_scalar_add", sgate_n[:, :], sgate_n[:, :], 1.0)
        R.op("v", "reciprocal", sgate_n[:, :], sgate_n[:, :])
        R.op("v", "tensor_copy", s1_n[:, :], es_n[:, 0::9])
        for e in range(1, 8):
            R.op("v", "tensor_tensor", s1_n[:, :], s1_n[:, :], es_n[:, e::9], ALU.add)
        R.op("v", "wait_ge", cRK.h, rk_ld)
        R.op("v", "tensor_copy", rank_n[:, :], rk_init[:, :])
        for a in range(8):
            for b in range(a + 1, 8):
                R.op("v", "tensor_tensor", m1v[:, :], es_n[:, b::9], es_n[:, a::9], ALU.is_gt)
                R.op("v", "tensor_tensor", rank_n[:, a::9], rank_n[:, a::9], m1v[:, :], ALU.add)
                R.op("v", "tensor_tensor", rank_n[:, b::9], rank_n[:, b::9], m1v[:, :],
                     ALU.subtract)
        R.op("v", "tensor_single_scalar", rank_n[:, :], rank_n[:, :], 2.0, ALU.is_lt)
        R.op("v", "tensor_tensor", es_n[:, :], es_n[:, :], rank_n[:, :], ALU.mult)
        R.op("v", "reciprocal", s1_n[:, :], s1_n[:, :])
        for e in range(8):
            R.op("v", "tensor_tensor", C_nat[:, e::8], es_n[:, e::9], s1_n[:, :], ALU.mult)
        R.mark("v", sV)
        barrier()
        phase_done("router")
        rsr.close()

        v_z1 = compute_z1()
        barrier()
        psA2.close()

        psB2 = ExitStack()
        PS["proj"] = [psB2.enter_context(nc.psum_tensor("ps_pA2", [128, 1024], F32)),
                      psB2.enter_context(nc.psum_tensor("ps_pB2", [128, 1024], F32))]
        PS["vt"] = psB2.enter_context(nc.psum_tensor("ps_vt2", [128, 512], BF16))
        PS["scf"] = psB2.enter_context(nc.psum_tensor("ps_scf2", [128, 512], F32))
        PS["scb"] = psB2.enter_context(nc.psum_tensor("ps_scb2", [128, 1024], BF16))
        G["vt"] = G["scf"] = G["scb"] = 0
        ogh_store = [0, 0]

        for h in range(H):
            slq, ldq = load_whead(wq_h.rearrange("(kb p) m -> p kb m", p=128)[:, :, h * DH:(h + 1) * DH])
            slk, ldk = load_whead(wk_h.rearrange("(kb p) m -> p kb m", p=128)[:, :, h * DH:(h + 1) * DH])
            slv, ldv = load_whead(wv_h.rearrange("(kb p) m -> p kb m", p=128)[:, :, h * DH:(h + 1) * DH])
            slg, ldg = load_whead(wgate_h.rearrange("(kb p) m -> p kb m", p=128)[:, :, h * DH:(h + 1) * DH])
            load_wgk2(h)
            mm_proj(slq, ldq, qT, mask=False)
            mm_proj(slk, ldk, kT, mask=False)
            mm_proj_v(slv, ldv)
            mm_proj_gate(slg, ldg)
            v_gk = mm_z2_gk(v_z1)
            R.op("v", "tensor_copy", ccol[:, CTMP * 16:CTMP * 16 + 16], Gg[:, 0::128])
            R.op("v", "tensor_scalar", ccol[:, CB1 * 16:CB1 * 16 + 16],
                 ccol[:, CPRV * 16:CPRV * 16 + 16], -1.0 / 16.0, LNSCALE, ALU.mult, ALU.add)
            R.op("v", "tensor_scalar_mul", ccol[:, CB2 * 16:CB2 * 16 + 16],
                 ccol[:, CTMP * 16:CTMP * 16 + 16], 1.0 / 16.0)
            R.op("v", "tensor_tensor", ccol[:, CC1 * 16:CC1 * 16 + 16],
                 ccol[:, CPRV * 16:CPRV * 16 + 16], ccol[:, CTMP * 16:CTMP * 16 + 16],
                 ALU.subtract)
            R.op("v", "tensor_tensor", ccol[:, CC2 * 16:CC2 * 16 + 16],
                 ccol[:, CEND * 16:CEND * 16 + 16], ccol[:, CTMP * 16:CTMP * 16 + 16],
                 ALU.subtract)
            R.op("v", "tensor_tensor", ccol[:, CDEC * 16:CDEC * 16 + 16],
                 ccol[:, CEND * 16:CEND * 16 + 16], ccol[:, CPRV * 16:CPRV * 16 + 16],
                 ALU.subtract)
            if G["chunk_pe"]:
                R.op("v", "wait_ge", sT.h, G["chunk_pe"])
            R.op("v", "tensor_copy", Sf[:, :], S_in[:, h * DH:(h + 1) * DH])
            R.op("v", "tensor_copy", Sbf[:, :], Sf[:, :])
            R.mark("v", sV)
            v_cols = sV.n
            last_sbf = v_cols
            R.op("s", "wait_ge", sV.h, v_cols)
            R.op("s", "activation", ccol[:, CC1 * 16:CC1 * 16 + 16],
                 ccol[:, CC1 * 16:CC1 * 16 + 16], AF.Exp, scale=1.0 / 16.0)
            R.op("s", "activation", ccol[:, CC2 * 16:CC2 * 16 + 16],
                 ccol[:, CC2 * 16:CC2 * 16 + 16], AF.Exp, scale=1.0 / 16.0)
            R.op("s", "activation", ccol[:, CDEC * 16:CDEC * 16 + 16],
                 ccol[:, CDEC * 16:CDEC * 16 + 16], AF.Exp, scale=1.0 / 16.0)
            R.mark("s", sS)
            s_cols = sS.n
            ob = h % 2
            if ogh_store[ob]:
                R.op("v", "wait_ge", cOG[ob].h, ogh_store[ob])
            for j in range(NCH):
                R.op("s", "wait_ge", sV.h, v_cols)
                if G["ebuf_v"]:
                    R.op("s", "wait_ge", sV.h, G["ebuf_v"])
                R.op("s", "activation", e1buf[:, :], Gg[:, j * 128:(j + 1) * 128], AF.Exp,
                     scale=1.0 / 16.0, bias=ccol[:, CB1 * 16 + j:CB1 * 16 + j + 1])
                R.op("s", "activation", e2buf[:, :], Gg[:, j * 128:(j + 1) * 128], AF.Exp,
                     scale=-1.0 / 16.0, bias=ccol[:, CB2 * 16 + j:CB2 * 16 + j + 1])
                R.mark("s", sS)
                s_e = sS.n
                R.op("v", "wait_ge", sS.h, s_e)
                R.op("v", "wait_ge", sS.h, s_cols)
                if G["chunk_pe"]:
                    R.op("v", "wait_ge", sT.h, G["chunk_pe"])
                R.op("v", "tensor_tensor", qgbf[:, :], qT[:, j * 128:(j + 1) * 128],
                     e1buf[:, :], ALU.mult)
                R.op("v", "tensor_scalar_mul", qhbf[:, :], qgbf[:, :],
                     ccol[:, CC1 * 16 + j:CC1 * 16 + j + 1])
                R.op("v", "tensor_tensor", khbf[:, :], kT[:, j * 128:(j + 1) * 128],
                     e2buf[:, :], ALU.mult)
                R.op("v", "tensor_scalar_mul", kwbf[:, :], khbf[:, :],
                     ccol[:, CC2 * 16 + j:CC2 * 16 + j + 1])
                R.mark("v", sV)
                v_gates = sV.n
                G["ebuf_v"] = v_gates
                R.op("t", "wait_ge", sV.h, v_gates)
                if G["scf"]:
                    R.op("t", "wait_ge", sV.h, G["scf"])
                R.op("t", "matmul", PS["scf"][:, 0:128], khbf[:, :], qhbf[:, :],
                     start=True, stop=True)
                R.mark("t", sT)
                t_at = sT.n
                if G["scb"]:
                    R.op("t", "wait_ge", sV.h, G["scb"])
                R.op("t", "transpose", PS["scb"][:, 0:128], kwbf[:, :], eye_bs[:, :])
                R.mark("t", sT)
                t_kwt = sT.n
                R.op("v", "wait_ge", sT.h, t_at)
                R.op("v", "tensor_tensor", atbf[:, :], PS["scf"][:, 0:128], triu_s[:, :],
                     ALU.mult)
                R.mark("v", sV)
                v_at = sV.n
                R.op("v", "wait_ge", sT.h, t_kwt)
                R.op("v", "tensor_copy", kwn[:, :], PS["scb"][:, 0:128])
                R.mark("v", sV)
                v_kwn = sV.n
                R.op("t", "wait_ge", sV.h, v_at)
                R.op("t", "wait_ge", sV.h, last_sbf)
                R.op("t", "matmul", PS["scf"][:, 128:256], atbf[:, :],
                     vnat[:, j * 128:(j + 1) * 128], start=True, stop=False)
                R.op("t", "matmul", PS["scf"][:, 128:256], qgbf[:, :], Sbf[:, :],
                     start=False, stop=True)
                R.mark("t", sT)
                t_o = sT.n
                R.op("t", "wait_ge", sV.h, v_kwn)
                R.op("t", "matmul", PS["scf"][:, 256:384], kwn[:, :],
                     vnat[:, j * 128:(j + 1) * 128], start=True, stop=True)
                R.mark("t", sT)
                t_su = sT.n
                R.op("v", "wait_ge", sT.h, t_su)
                R.op("v", "scalar_tensor_tensor", Sf[:, :], Sf[:, :],
                     ccol[:, CDEC * 16 + j:CDEC * 16 + j + 1], PS["scf"][:, 256:384],
                     ALU.mult, ALU.add)
                R.op("v", "tensor_copy", Sbf[:, :], Sf[:, :])
                R.mark("v", sV)
                last_sbf = sV.n
                R.op("v", "wait_ge", sT.h, t_o)
                R.op("v", "tensor_copy", of32[:, :], PS["scf"][:, 128:256])
                R.op("v", "scalar_tensor_tensor", tr128[:, :], of32[:, :], 1.0, of32[:, :],
                     ALU.mult, ALU.mult, accum_out=stat[:, 6:7])
                R.mark("v", sV)
                v_ssq = sV.n
                G["scf"] = v_ssq
                R.op("s", "wait_ge", sV.h, v_ssq)
                R.op("s", "activation", stat[:, 7:8], stat[:, 6:7], AF.Ln,
                     bias=cols[:, CE128:CE128 + 1])
                R.op("s", "activation", stat[:, 6:7], stat[:, 7:8], AF.Exp, scale=-0.5)
                R.mark("s", sS)
                s_rs = sS.n
                R.op("v", "wait_ge", sS.h, s_rs)
                R.op("v", "tensor_scalar_mul", osc[:, :], of32[:, :], stat[:, 6:7])
                R.mark("v", sV)
                v_osc = sV.n
                R.op("t", "wait_ge", sV.h, v_osc)
                if G["scb"]:
                    R.op("t", "wait_ge", sV.h, G["scb"])
                R.op("t", "transpose", PS["scb"][:, 128:256], osc[:, :], eye_bs[:, :])
                R.mark("t", sT)
                t_ot = sT.n
                G["chunk_pe"] = t_ot
                R.op("v", "wait_ge", sT.h, t_ot)
                R.op("v", "scalar_tensor_tensor", ogh[ob][:, j * 128:(j + 1) * 128],
                     PS["scb"][:, 128:256], gnorm_c[:, 0:1], sgT[:, j * 128:(j + 1) * 128],
                     ALU.mult, ALU.mult)
                R.mark("v", sV)
                G["scb"] = sV.n
            R.op("y", "wait_ge", sV.h, sV.n)
            R.op("y", "dma_start", og_hbm[h * DH:(h + 1) * DH, :],
                 ogh[ob][:, :]).then_inc(cOG[ob].bump(16), 16)
            ogh_store[ob] = cOG[ob].n
            G["head_s_guard"] = sS.n
            G["sgT_guard"] = sV.n

        barrier()
        phase_done("attn")
        psB2.close()
        rs1.close()
        rs0.close()

        psW = ExitStack()
        PS["proj"] = [psW.enter_context(nc.psum_tensor("ps_woA", [128, 1024], F32)),
                      psW.enter_context(nc.psum_tensor("ps_woB", [128, 1024], F32))]
        rsW = ExitStack()

        def sbrw(name, shape, dt):
            return rsW.enter_context(nc.sbuf_tensor(name, list(shape), dt, side="right"))

        wo_b = sbrw("wo_b", (128, KB * D), BF16)
        ogsl = [sbrw("ogsl0", (128, H * 128), BF16), sbrw("ogsl1", (128, H * 128), BF16)]
        xtw = [sbrw("xtw0", (128, D), F32), sbrw("xtw1", (128, D), F32)]
        o1sb = [sbrw("o1sb0", (128, 1024), F32), sbrw("o1sb1", (128, 1024), F32)]
        R.op("g", "dma_start", wo_b[:, :].rearrange("p (kb n) -> p kb n", kb=KB),
             wo_h.rearrange("(kb p) n -> p kb n", p=128)).then_inc(cWO.bump(16), 16)
        wo_ld = cWO.n
        G["proj_v"] = [0, 0]
        G["proj_s"] = [0, 0]
        ogsl_tok = [0, 0]
        xtw_tok = [0, 0]
        o1_store = [0, 0]
        for tb in range(NT):
            buf = tb % 2
            if ogsl_tok[buf]:
                R.op("y", "wait_ge", sT.h, ogsl_tok[buf])
            R.op("y", "dma_start", ogsl[buf][:, :].rearrange("p (hh m) -> p hh m", hh=H),
                 og_hbm[:, tb * 128:(tb + 1) * 128].rearrange("(hh p) m -> p hh m", p=128)
                 ).then_inc(cOS[buf].bump(16), 16)
            og_ld = cOS[buf].n
            if xtw_tok[buf]:
                R.op("y", "wait_ge", sV.h, xtw_tok[buf])
            R.op("y", "dma_start", xtw[buf][:, :],
                 xseg[tb * 128:(tb + 1) * 128, :]).then_inc(cXW[buf].bump(16), 16)
            x_ld = cXW[buf].n
            for half in range(2):
                pb = half
                R.op("t", "wait_ge", cWO.h, wo_ld)
                R.op("t", "wait_ge", cOS[buf].h, og_ld)
                proj_wait(pb)
                for hh in range(H):
                    for n in range(2):
                        woff = hh * D + half * 1024 + n * 512
                        R.op("t", "matmul", PS["proj"][pb][:, n * 512:(n + 1) * 512],
                             ogsl[buf][:, hh * 128:(hh + 1) * 128],
                             wo_b[:, woff:woff + 512], start=(hh == 0), stop=(hh == H - 1))
                R.mark("t", sT)
                t_wo = sT.n
                if half == 1:
                    ogsl_tok[buf] = t_wo
                R.op("v", "wait_ge", sT.h, t_wo)
                R.op("v", "wait_ge", cXW[buf].h, x_ld)
                if o1_store[half]:
                    R.op("v", "wait_ge", cO1[half].h, o1_store[half])
                R.op("v", "tensor_tensor", o1sb[half][:, :], PS["proj"][pb][:, :],
                     xtw[buf][:, half * 1024:(half + 1) * 1024], ALU.add)
                R.mark("v", sV)
                G["proj_v"][pb] = sV.n
                xtw_tok[buf] = sV.n
                R.op("y", "wait_ge", sV.h, sV.n)
                R.op("y", "dma_start",
                     out1_hbm[tb * 128:(tb + 1) * 128, half * 1024:(half + 1) * 1024],
                     o1sb[half][:, :]).then_inc(cO1[half].bump(16), 16)
                o1_store[half] = cO1[half].n

        barrier()
        phase_done("wo")
        psW.close()
        rsW.close()

        psF = ExitStack()
        PS["proj"] = [psF.enter_context(nc.psum_tensor("ps_gA", [128, 1024], F32)),
                      psF.enter_context(nc.psum_tensor("ps_gB", [128, 1024], F32))]
        PS["u"] = [psF.enter_context(nc.psum_tensor("ps_uA", [128, 1024], F32)),
                   psF.enter_context(nc.psum_tensor("ps_uB", [128, 1024], F32))]
        rsF = ExitStack()

        def sbrf(name, shape, dt):
            return rsF.enter_context(nc.sbuf_tensor(name, list(shape), dt, side="right"))

        wgu = [sbrf(f"wgu{i}", (128, KB * 128), BF16) for i in range(4)]
        sgil = sbrf("sgil", (128, 1024), F32)
        hsb = [sbrf("hsb0", (128, T), BF16), sbrf("hsb1", (128, T), BF16)]
        wgu_tok = [0] * 4
        hsb_store = [0, 0]
        G["proj_v"] = [0, 0]
        G["proj_s"] = [0, 0]
        u_guard = [0, 0]
        sgil_guard = 0

        def gu_src(e, fb, which):
            if e < 8:
                t = ewg_h if which == 0 else ewu_h
                return t[e, :, :].rearrange("(kb p) m -> p kb m", p=128)[:, :, fb * 128:(fb + 1) * 128]
            t = sgw_h if which == 0 else suw_h
            off = (e - 8) * F + fb * 128
            return t.rearrange("(kb p) m -> p kb m", p=128)[:, :, off:off + 128]

        idx = 0
        for e in range(NE):
            for fb in range(FB):
                sg_slot = (2 * idx) % 4
                su_slot = (2 * idx + 1) % 4
                if wgu_tok[sg_slot]:
                    R.op("g", "wait_ge", sT.h, wgu_tok[sg_slot])
                R.op("g", "dma_start", wgu[sg_slot][:, :].rearrange("p (kb m) -> p kb m", kb=KB),
                     gu_src(e, fb, 0)).then_inc(cGU[sg_slot].bump(16), 16)
                ld_g = cGU[sg_slot].n
                if wgu_tok[su_slot]:
                    R.op("g", "wait_ge", sT.h, wgu_tok[su_slot])
                R.op("g", "dma_start", wgu[su_slot][:, :].rearrange("p (kb m) -> p kb m", kb=KB),
                     gu_src(e, fb, 1)).then_inc(cGU[su_slot].bump(16), 16)
                ld_u = cGU[su_slot].n
                hb = idx % 2
                for half in range(2):
                    R.op("t", "wait_ge", cGU[sg_slot].h, ld_g)
                    if G["proj_s"][half]:
                        R.op("t", "wait_ge", sS.h, G["proj_s"][half])
                    for kb in range(KB):
                        for n in range(2):
                            off = kb * T + half * 1024 + n * 512
                            R.op("t", "matmul", PS["proj"][half][:, n * 512:(n + 1) * 512],
                                 wgu[sg_slot][:, kb * 128:(kb + 1) * 128],
                                 xnT[:, off:off + 512], start=(kb == 0), stop=(kb == KB - 1))
                    R.mark("t", sT)
                    t_g = sT.n
                    R.op("t", "wait_ge", cGU[su_slot].h, ld_u)
                    if u_guard[half]:
                        R.op("t", "wait_ge", sV.h, u_guard[half])
                    for kb in range(KB):
                        for n in range(2):
                            off = kb * T + half * 1024 + n * 512
                            R.op("t", "matmul", PS["u"][half][:, n * 512:(n + 1) * 512],
                                 wgu[su_slot][:, kb * 128:(kb + 1) * 128],
                                 xnT[:, off:off + 512], start=(kb == 0), stop=(kb == KB - 1))
                    R.mark("t", sT)
                    t_u = sT.n
                    if half == 1:
                        wgu_tok[sg_slot] = t_g
                        wgu_tok[su_slot] = t_u
                    R.op("s", "wait_ge", sT.h, t_g)
                    if sgil_guard:
                        R.op("s", "wait_ge", sV.h, sgil_guard)
                    R.op("s", "activation", sgil[:, :], PS["proj"][half][:, :], AF.Silu)
                    R.mark("s", sS)
                    s_sil = sS.n
                    G["proj_s"][half] = s_sil
                    R.op("v", "wait_ge", sS.h, s_sil)
                    R.op("v", "wait_ge", sT.h, t_u)
                    if hsb_store[hb]:
                        R.op("v", "wait_ge", cH[hb].h, hsb_store[hb])
                    R.op("v", "tensor_tensor", hsb[hb][:, half * 1024:(half + 1) * 1024],
                         PS["u"][half][:, :], sgil[:, :], ALU.mult)
                    R.mark("v", sV)
                    u_guard[half] = sV.n
                    sgil_guard = sV.n
                R.op("y", "wait_ge", sV.h, sV.n)
                R.op("y", "dma_start", h_hbm[e * F + fb * 128: e * F + (fb + 1) * 128, :],
                     hsb[hb][:, :]).then_inc(cH[hb].bump(16), 16)
                hsb_store[hb] = cH[hb].n
                idx += 1

        barrier()
        phase_done("ffna")
        psF.close()
        rsF.close()
        xnT_es.close()

        psM = ExitStack()
        ps_moe = [psM.enter_context(nc.psum_tensor("ps_m0", [128, 512], F32)),
                  psM.enter_context(nc.psum_tensor("ps_m1", [128, 512], F32))]
        rsB = ExitStack()

        def sbrb(name, shape, dt):
            return rsB.enter_context(nc.sbuf_tensor(name, list(shape), dt, side="right"))

        acc = sbrb("acc", (128, 8 * D), F32)
        wd_b = [sbrb("wd0", (128, FB * D), BF16), sbrb("wd1", (128, FB * D), BF16)]
        hsl = [sbrb("hsl0", (128, FB * 128), BF16), sbrb("hsl1", (128, FB * 128), BF16)]

        def wd_src(e):
            if e < 8:
                return ewd_h[e, :, :].rearrange("(fb p) n -> p fb n", p=128)
            return sdw_h[(e - 8) * F:(e - 7) * F, :].rearrange("(fb p) n -> p fb n", p=128)

        wd_tok = [0, 0]
        hsl_tok = [0, 0]
        moe_guard = [0, 0]
        acc_store = 0
        for sub in range(2):
            R.op("y", "wait_ge", sV.h, sV.n)
            if acc_store:
                R.op("y", "wait_ge", cY.h, acc_store)
            R.op("y", "dma_start", acc[:, :].rearrange("p (tb d) -> p tb d", tb=8),
                 out1_hbm[sub * 1024:(sub + 1) * 1024, :].rearrange("(tb p) d -> p tb d", p=128)
                 ).then_inc(cACC.bump(16), 16)
            acc_ld = cACC.n
            for e in range(NE):
                sl = e % 2
                if wd_tok[sl]:
                    R.op("g", "wait_ge", sT.h, wd_tok[sl])
                R.op("g", "dma_start", wd_b[sl][:, :].rearrange("p (fb n) -> p fb n", fb=FB),
                     wd_src(e)).then_inc(cWD[sl].bump(16), 16)
                ld_wd = cWD[sl].n
                for tb8 in range(8):
                    tb = sub * 8 + tb8
                    hb = tb8 % 2
                    if hsl_tok[hb]:
                        R.op("y", "wait_ge", sT.h, hsl_tok[hb])
                    R.op("y", "dma_start", hsl[hb][:, :].rearrange("p (fb m) -> p fb m", fb=FB),
                         h_hbm[e * F:(e + 1) * F, tb * 128:(tb + 1) * 128].rearrange(
                             "(fb p) m -> p fb m", p=128)).then_inc(cHL[hb].bump(16), 16)
                    ld_h = cHL[hb].n
                    for dc in range(4):
                        pb = (tb8 * 4 + dc) % 2
                        R.op("t", "wait_ge", cWD[sl].h, ld_wd)
                        R.op("t", "wait_ge", cHL[hb].h, ld_h)
                        if moe_guard[pb]:
                            R.op("t", "wait_ge", sV.h, moe_guard[pb])
                        for fb in range(FB):
                            R.op("t", "matmul", ps_moe[pb][:, :],
                                 hsl[hb][:, fb * 128:(fb + 1) * 128],
                                 wd_b[sl][:, fb * D + dc * 512: fb * D + (dc + 1) * 512],
                                 start=(fb == 0), stop=(fb == FB - 1))
                        R.mark("t", sT)
                        t_moe = sT.n
                        if dc == 3:
                            hsl_tok[hb] = t_moe
                            if tb8 == 7:
                                wd_tok[sl] = t_moe
                        wc = C_nat[:, tb * 8 + e: tb * 8 + e + 1] if e < 8 else sgate_n[:, tb:tb + 1]
                        R.op("v", "wait_ge", sT.h, t_moe)
                        if e == 0 and dc == 0 and tb8 == 0:
                            R.op("v", "wait_ge", cACC.h, acc_ld)
                        R.op("v", "scalar_tensor_tensor",
                             acc[:, tb8 * D + dc * 512: tb8 * D + (dc + 1) * 512],
                             ps_moe[pb][:, :], wc,
                             acc[:, tb8 * D + dc * 512: tb8 * D + (dc + 1) * 512],
                             ALU.mult, ALU.add)
                        R.mark("v", sV)
                        moe_guard[pb] = sV.n
            R.op("y", "wait_ge", sV.h, sV.n)
            R.op("y", "dma_start",
                 y_out[sub * 1024:(sub + 1) * 1024, :].rearrange("(tb p) d -> p tb d", p=128),
                 acc[:, :].rearrange("p (tb d) -> p tb d", tb=8)).then_inc(cY.bump(16), 16)
            acc_store = cY.n

        R.op("y", "wait_ge", cY.h, cY.n)
    except _Done:
        R.op("y", "dma_start", y_out, xseg).then_inc(cY.bump(16), 16)
        R.op("y", "wait_ge", cY.h, cY.n)

    R.selfsem["v"] = sV.h
    R.selfsem["s"] = sS.h
    with nc.Block() as block:
        @block.tensor
        def _(t):
            R.replay("t", t)

        @block.vector
        def _(v):
            R.replay("v", v)

        @block.scalar
        def _(s):
            R.replay("s", s)

        @block.gpsimd
        def _(g):
            R.replay("g", g)

        @block.sync
        def _(yq):
            R.replay("y", yq)

    if phase_limit == "full":
        psM.close()
        rsB.close()
        es.close()
    return nc


_NC_CACHE = []


def kernel(**inputs):
    x = np.ascontiguousarray(np.asarray(inputs["L0_attn_x"], dtype=np.float32))
    B, L, _ = x.shape
    if not _NC_CACHE:
        _NC_CACHE.append(build_nc())
    nc = _NC_CACHE[0]
    base = {
        "wq": np.ascontiguousarray(inputs["L0_attn_Wq"], np.float32),
        "wk": np.ascontiguousarray(inputs["L0_attn_Wk"], np.float32),
        "wv": np.ascontiguousarray(inputs["L0_attn_Wv"], np.float32),
        "wgk1": np.ascontiguousarray(inputs["L0_attn_Wgk1"], np.float32),
        "wgk2": np.ascontiguousarray(inputs["L0_attn_Wgk2"], np.float32),
        "bgk2": np.ascontiguousarray(inputs["L0_attn_bgk2"], np.float32),
        "wgate": np.ascontiguousarray(inputs["L0_attn_Wg"], np.float32),
        "gnw": np.ascontiguousarray(inputs["L0_attn_gnorm_w"], np.float32),
        "wo": np.ascontiguousarray(inputs["L0_attn_Wo"], np.float32),
        "wr": np.ascontiguousarray(inputs["L0_ffn_Wr"], np.float32),
        "wsg": np.ascontiguousarray(inputs["L0_ffn_Wsg"], np.float32),
        "ewg": np.ascontiguousarray(inputs["L0_ffn_Wg"], np.float32),
        "ewu": np.ascontiguousarray(inputs["L0_ffn_Wu"], np.float32),
        "ewd": np.ascontiguousarray(inputs["L0_ffn_Wd"], np.float32),
        "sgw": np.ascontiguousarray(inputs["L0_ffn_Sg"], np.float32),
        "suw": np.ascontiguousarray(inputs["L0_ffn_Su"], np.float32),
        "sdw": np.ascontiguousarray(inputs["L0_ffn_Sd"], np.float32),
    }
    in_maps = []
    for c in range(8):
        b, s = c // 2, c % 2
        m = dict(base)
        m["xseg"] = np.ascontiguousarray(x[b, s * T:(s + 1) * T, :])
        m["xpre"] = np.ascontiguousarray(x[b, 0:T, :])
        m["zcol"] = np.full((128, 1), 1.0 if s == 1 else 0.0, np.float32)
        in_maps.append(m)
    res = run_bass_kernel_spmd(nc, in_maps, core_ids=list(range(8)))
    out = np.empty((B, L, D), np.float32)
    for c in range(8):
        b, s = c // 2, c % 2
        out[b, s * T:(s + 1) * T, :] = res.results[c]["y"]
    return out


# version 02
# speedup vs baseline: 1.1062x; 1.1062x over previous
import numpy as np
from contextlib import ExitStack

import ml_dtypes
import concourse.bass as bass
import concourse.mybir as mybir
from concourse.bass_utils import run_bass_kernel_spmd

F32 = mybir.dt.float32
BF16 = mybir.dt.bfloat16
AF = mybir.ActivationFunctionType
ALU = mybir.AluOpType

T = 2048
D = 2048
H = 16
DH = 128
CH = 128
NCH = T // CH
KB = D // 128
NT = T // 128
F = 1408
FB = F // 128
NE = 12
FS = 5632
EPS = 1e-5
LNSCALE = float(np.log(np.float32(128.0) ** np.float32(-0.5)))
SQ2048 = float(np.sqrt(np.float32(2048.0)))
SQ128 = float(np.sqrt(np.float32(128.0)))

ENGS = ("t", "v", "s", "g", "y")


class _Inst:
    __slots__ = ("attr", "args", "kw", "incs")

    def __init__(self, attr, args, kw):
        self.attr, self.args, self.kw = attr, args, kw
        self.incs = []

    def then_inc(self, sem, v=1):
        self.incs.append((sem, v))
        return self


class Rec:

    SELF = ("v", "s")

    def __init__(self):
        self.ev = {e: [] for e in ENGS}
        self.cnt = {e: 0 for e in self.SELF}
        self.selfsem = {}

    def op(self, eng, attr, *args, **kw):
        inst = _Inst(attr, args, kw)
        self.ev[eng].append(inst)
        if eng in self.SELF and attr != "wait_ge":
            self.cnt[eng] += 1
        return inst

    def mark(self, eng, sem):
        if eng in self.SELF:
            return self.cnt[eng]
        for inst in reversed(self.ev[eng]):
            if inst.attr != "wait_ge":
                for (h, _v) in inst.incs:
                    if h is sem.h:
                        return sem.n
                sem.bump()
                inst.incs.append((sem.h, 1))
                return sem.n
        raise AssertionError(f"no real instruction to mark on {eng}")

    def replay(self, eng_name, eng_obj):
        if eng_name in self.SELF:
            sem = self.selfsem[eng_name]
            idx = 0
            for inst in self.ev[eng_name]:
                if inst.attr == "wait_ge":
                    getattr(eng_obj, inst.attr)(*inst.args, **inst.kw)
                    continue
                if idx > 0:
                    eng_obj.wait_ge(sem, idx)
                idx += 1
                real = getattr(eng_obj, inst.attr)(*inst.args, **inst.kw)
                real.then_inc(sem, 1)
                for (s, v) in inst.incs:
                    real.then_inc(s, v)
            return
        for inst in self.ev[eng_name]:
            real = getattr(eng_obj, inst.attr)(*inst.args, **inst.kw)
            for (sem, v) in inst.incs:
                real.then_inc(sem, v)


class Sem:
    def __init__(self, es, nc, name, rec=None, eng=None):
        self.h = es.enter_context(nc.semaphore(name))
        self._n = 0
        self._rec, self._eng = rec, eng

    @property
    def n(self):
        if self._rec is not None:
            return self._rec.cnt[self._eng]
        return self._n

    def bump(self, k=1):
        assert self._rec is None
        self._n += k
        return self.h


def build_nc(phase_limit="full"):
    nc = bass.Bass("TRN2", target_bir_lowering=False)

    class _Done(Exception):
        pass

    es = ExitStack()
    R = Rec()

    def din(name, shape, dt=F32):
        return nc.dram_tensor(name, list(shape), dt, kind="ExternalInput").ap()

    xseg = din("xseg", (T, D))
    xpre = din("xpre", (T, D))
    zcol_h = din("zcol", (128, 1))
    wq_h = din("wq", (D, D)); wk_h = din("wk", (D, D)); wv_h = din("wv", (D, D))
    wgk1_h = din("wgk1", (D, 16)); wgk2_h = din("wgk2", (16, D)); bgk2_h = din("bgk2", (D,))
    wgate_h = din("wgate", (D, D)); gnw_h = din("gnw", (DH,)); wo_h = din("wo", (D, D))
    wr_h = din("wr", (D, 8)); wsg_h = din("wsg", (D, 1))
    ewg_h = din("ewg", (8, D, F)); ewu_h = din("ewu", (8, D, F)); ewd_h = din("ewd", (8, F, D))
    sgw_h = din("sgw", (D, FS)); suw_h = din("suw", (D, FS)); sdw_h = din("sdw", (FS, D))

    y_out = nc.dram_tensor("y", [T, D], F32, kind="ExternalOutput").ap()
    h_hbm = nc.dram_tensor("h_scr", [NE * F, T], BF16, kind="Internal").ap()
    out1_hbm = nc.dram_tensor("out1_scr", [T, D], F32, kind="Internal").ap()
    og_hbm = nc.dram_tensor("og_scr", [H * DH, T], BF16, kind="Internal").ap()

    eye_f_c = nc.inline_tensor(np.eye(128, dtype=np.float32), "eye_f").ap()
    eye_b_c = nc.inline_tensor(np.eye(128, dtype=ml_dtypes.bfloat16), "eye_b").ap()
    triu_c = nc.inline_tensor(np.triu(np.ones((128, 128), np.float32)), "triu").ap()
    triu4_c = nc.inline_tensor(np.tile(np.triu(np.ones((128, 128), np.float32)), (1, 4)),
                               "triu4").ap()
    zeros_c = nc.inline_tensor(np.zeros((128, 512), np.float32), "zeros512").ap()
    rk0 = np.zeros((128, NT * 9), np.float32)
    for j in range(NT):
        for e in range(8):
            rk0[:, 9 * j + e] = e
    rank0_c = nc.inline_tensor(rk0, "rank0").ap()

    sT = Sem(es, nc, "sT")
    sV = Sem(es, nc, "sV", rec=R, eng="v")
    sS = Sem(es, nc, "sS", rec=R, eng="s")
    cC = Sem(es, nc, "cC"); cRK = Sem(es, nc, "cRK")
    cX = [Sem(es, nc, "cX0"), Sem(es, nc, "cX1")]
    cO1 = [Sem(es, nc, "cO1a"), Sem(es, nc, "cO1b")]
    cOG = [Sem(es, nc, "cOGa"), Sem(es, nc, "cOGb")]
    cOS = [Sem(es, nc, "cOSa"), Sem(es, nc, "cOSb")]
    cXW = [Sem(es, nc, "cXWa"), Sem(es, nc, "cXWb")]
    cH = [Sem(es, nc, "cHa"), Sem(es, nc, "cHb")]
    cHL = [Sem(es, nc, "cHLa"), Sem(es, nc, "cHLb")]
    cACC = Sem(es, nc, "cACC"); cY = Sem(es, nc, "cY")
    cRT = Sem(es, nc, "cRT")
    cWR = [Sem(es, nc, f"cWR{i}") for i in range(5)]
    cGK = Sem(es, nc, "cGK")
    cGU = [Sem(es, nc, f"cGU{i}") for i in range(4)]
    cWD = [Sem(es, nc, "cWD0"), Sem(es, nc, "cWD1")]
    cWO = Sem(es, nc, "cWO")
    DMASEMS = ([cC, cRK, cACC, cY, cRT, cGK, cWO] + cX + cO1 + cOG + cOS
               + cXW + cH + cHL + cWR + cGU + cWD)

    def sbl(name, shape, dt):
        return es.enter_context(nc.sbuf_tensor(name, list(shape), dt, side="left"))

    eye_fs = sbl("eye_fs", (128, 128), F32)
    eye_bs = sbl("eye_bs", (128, 128), BF16)
    triu4_s = sbl("triu4_s", (128, 512), F32)
    zeros_s = sbl("zeros_s", (128, 512), F32)
    cols = sbl("cols", (128, 8), F32)
    gnorm_c = sbl("gnorm_c", (128, 1), F32)
    C_nat = sbl("C_nat", (128, NT * 8), F32)
    sgate_n = sbl("sgate_n", (128, NT), F32)
    S_in = sbl("S_in", (128, H * DH), F32)
    z1b = sbl("z1b", (32, T), BF16)
    wgk1_b = sbl("wgk1_b", (128, KB * 16), BF16)
    wgk2_b = sbl("wgk2_b", (32, DH), BF16)
    wr9_hi = sbl("wr9_hi", (128, KB * 9), BF16)
    wr9_lo = sbl("wr9_lo", (128, KB * 9), BF16)
    stat = sbl("stat", (128, 8), F32)
    xnT_es = ExitStack()
    xnT = xnT_es.enter_context(nc.sbuf_tensor("xnT", [128, KB * T], BF16, side="left"))

    CZ, CE2048, CE128, CONE = 0, 1, 2, 3

    R.op("y", "dma_start", eye_fs[:, :], eye_f_c).then_inc(cC.bump(16), 16)
    R.op("y", "dma_start", eye_bs[:, :], eye_b_c).then_inc(cC.bump(16), 16)
    R.op("y", "dma_start", triu4_s[:, :], triu4_c).then_inc(cC.bump(16), 16)
    R.op("y", "dma_start", zeros_s[:, :], zeros_c).then_inc(cC.bump(16), 16)
    R.op("y", "dma_start", cols[:, CZ:CZ + 1], zcol_h).then_inc(cC.bump(16), 16)
    R.op("y", "dma_start", gnorm_c[:, 0:1],
         gnw_h.rearrange("(p o) -> p o", o=1)).then_inc(cC.bump(16), 16)
    consts_ld = cC.n
    R.op("v", "memset", cols[:, CE2048:CE2048 + 1], 2048.0 * EPS)
    R.op("v", "memset", cols[:, CE128:CE128 + 1], 128.0 * EPS)
    R.op("v", "memset", cols[:, CONE:CONE + 1], 1.0)
    R.op("v", "wait_ge", cC.h, consts_ld)
    R.op("v", "tensor_scalar_mul", gnorm_c[:, 0:1], gnorm_c[:, 0:1], SQ128)
    R.mark("v", sV)

    rs0 = ExitStack()
    wr9_f = rs0.enter_context(nc.sbuf_tensor("wr9_f", [128, KB * 9], F32, side="right"))
    wrf = rs0.enter_context(nc.sbuf_tensor("wrf", [128, KB * 8], F32, side="right"))
    wsgT = rs0.enter_context(nc.sbuf_tensor("wsgT", [16, 128], F32, side="right"))
    R.op("g", "dma_start", wrf[:, :].rearrange("p (kb e) -> p kb e", kb=KB),
         wr_h.rearrange("(kb p) e -> p kb e", p=128)).then_inc(cRT.bump(16), 16)
    R.op("g", "dma_start", wsgT[:, :],
         wsg_h.rearrange("(kb p) o -> kb (p o)", p=128)).then_inc(cRT.bump(16), 16)
    R.op("g", "dma_start", wgk1_b[:, :].rearrange("p (kb e) -> p kb e", kb=KB),
         wgk1_h.rearrange("(kb p) e -> p kb e", p=128)).then_inc(cRT.bump(16), 16)
    wr9_ld = cRT.n

    phase_done_cb = [None]

    def phase_done(name):
        if phase_limit == name:
            raise _Done()
    phase_done_cb[0] = phase_done

    def barrier():
        R.mark("t", sT)
        R.mark("v", sV)
        R.mark("s", sS)
        waits = [(sT.h, sT.n), (sV.h, sV.n), (sS.h, sS.n)] + [
            (d.h, d.n) for d in DMASEMS if d.n > 0]
        for e in ENGS:
            for (hh, vv) in waits:
                if vv > 0:
                    R.op(e, "wait_ge", hh, vv)

    rs1 = ExitStack()

    def sbr1(name, shape, dt):
        return rs1.enter_context(nc.sbuf_tensor(name, list(shape), dt, side="right"))

    wring = [sbr1(f"wring{i}", (128, KB * 128), BF16) for i in range(4)]
    gku = sbr1("gku", (128, T), F32)
    Gg = sbr1("Gg", (128, T), F32)
    tmpA = sbr1("tmpA", (128, 256), F32)
    tmpB = sbr1("tmpB", (128, 256), F32)
    tmpC = sbr1("tmpC", (128, 256), F32)
    kT = sbr1("kT", (128, T), BF16)
    vnat = sbr1("vnat", (128, T), BF16)
    tmpv = sbr1("tmpv", (128, 512), BF16)
    ccol = sbr1("ccol", (128, 16 * 8), F32)
    CEND, CPRV, CB1, CB2, CC1, CC2, CDEC, CTMP = range(8)
    E2all = sbr1("E2all", (128, T), F32)
    kwall = sbr1("kwall", (128, T), BF16)
    kwnall = sbr1("kwnall", (128, T), BF16)
    Sf = sbr1("Sf", (128, 128), F32)
    tr128 = sbr1("tr128", (128, 128), BF16)
    sscol = sbr1("sscol", (128, 16), F32)
    rstdall = sbr1("rstdall", (128, 16), F32)
    xnf = gku
    sqtrash = Gg
    rsLN = ExitStack()
    xt = [rsLN.enter_context(nc.sbuf_tensor("xt0", [128, D], F32, side="right")),
          rsLN.enter_context(nc.sbuf_tensor("xt1", [128, D], F32, side="right"))]
    loT = rsLN.enter_context(nc.sbuf_tensor("loT", [128, 512], BF16, side="right"))

    G = {
        "proj_v": [0, 0], "proj_s": [0, 0], "vt": 0, "vt_t": 0, "scf": 0, "scb": 0,
        "wring_tok": [0] * 4, "wslot": 0, "wgk2_tok": 0, "wgk2_ld": 0,
        "pstr_s": [0, 0], "pstr_v": [0, 0], "pstr_t": [0, 0],
        "xnf_t": 0, "lo_t": 0, "head_s_guard": 0, "chunk_pe": 0,
        "ebuf_v": 0, "pkw_pe": 0, "pkw_v": 0,
    }
    PS = {}

    def ln_phase(x_src, is_seg):
        first_bank = [True, True, True, True]
        for i in range(NT):
            buf = i % 2
            if G[f"xt_guard{buf}"]:
                R.op("y", "wait_ge", sV.h, G[f"xt_guard{buf}"])
            R.op("y", "dma_start", xt[buf][:, :],
                 x_src[i * 128:(i + 1) * 128, :]).then_inc(cX[buf].bump(16), 16)
            x_ld = cX[buf].n
            R.op("v", "wait_ge", cX[buf].h, x_ld)
            R.op("v", "tensor_scalar", sqtrash[:, :], xt[buf][:, :], 1.0, 0.0, ALU.mult,
                 ALU.add, accum_out=stat[:, 0:1])
            R.op("v", "tensor_scalar_mul", stat[:, 1:2], stat[:, 0:1], -1.0 / 2048.0)
            R.mark("v", sV)
            v_nm = sV.n
            R.op("s", "wait_ge", sV.h, v_nm)
            R.op("s", "activation", sqtrash[:, :], xt[buf][:, :], AF.Square,
                 bias=stat[:, 1:2], accum_out=stat[:, 2:3])
            R.op("s", "activation", stat[:, 3:4], stat[:, 2:3], AF.Ln,
                 bias=cols[:, CE2048:CE2048 + 1])
            R.op("s", "activation", stat[:, 4:5], stat[:, 3:4], AF.Exp, scale=-0.5)
            R.mark("s", sS)
            s_r = sS.n
            R.op("v", "wait_ge", sS.h, s_r)
            R.op("v", "tensor_scalar_mul", stat[:, 5:6], stat[:, 4:5], SQ2048)
            if G["xnf_t"]:
                R.op("v", "wait_ge", sT.h, G["xnf_t"])
            R.op("v", "tensor_scalar", xnf[:, :], xt[buf][:, :], stat[:, 1:2],
                 stat[:, 5:6], ALU.add, ALU.mult)
            R.mark("v", sV)
            v_xn = sV.n
            G[f"xt_guard{buf}"] = v_xn
            for grp in range(4):
                pb = grp % 2
                R.op("t", "wait_ge", sV.h, v_xn)
                if G["pstr_s"][pb]:
                    R.op("t", "wait_ge", sS.h, G["pstr_s"][pb])
                if G["pstr_v"][pb]:
                    R.op("t", "wait_ge", sV.h, G["pstr_v"][pb])
                for q in range(4):
                    kb = grp * 4 + q
                    R.op("t", "transpose", PS["tr"][pb][:, q * 128:(q + 1) * 128],
                         xnf[:, kb * 128:(kb + 1) * 128], eye_fs[:, :])
                R.mark("t", sT)
                t_tr = sT.n
                if grp == 3:
                    G["xnf_t"] = t_tr
                R.op("s", "wait_ge", sT.h, t_tr)
                for q in range(4):
                    kb = grp * 4 + q
                    R.op("s", "copy", xnT[:, kb * T + i * 128: kb * T + (i + 1) * 128],
                         PS["tr"][pb][:, q * 128:(q + 1) * 128])
                R.mark("s", sS)
                G["pstr_s"][pb] = sS.n
                if is_seg:
                    R.op("v", "wait_ge", sS.h, sS.n)
                    if G["lo_t"]:
                        R.op("v", "wait_ge", sT.h, G["lo_t"])
                    for q in range(4):
                        kb = grp * 4 + q
                        R.op("v", "tensor_tensor", loT[:, q * 128:(q + 1) * 128],
                             PS["tr"][pb][:, q * 128:(q + 1) * 128],
                             xnT[:, kb * T + i * 128: kb * T + (i + 1) * 128], ALU.subtract)
                    R.mark("v", sV)
                    v_lo = sV.n
                    G["pstr_v"][pb] = v_lo
                    bank_i = i // 4
                    R.op("t", "wait_ge", sV.h, v_lo)
                    for q in range(4):
                        kb = grp * 4 + q
                        st = first_bank[bank_i]
                        first_bank[bank_i] = False
                        hi_ap = xnT[:, kb * T + i * 128: kb * T + (i + 1) * 128]
                        last = (i % 4 == 3) and (kb == KB - 1)
                        R.op("t", "matmul", PS["lg"][0:9, i * 128:(i + 1) * 128],
                             wr9_hi[:, kb * 9:(kb + 1) * 9], hi_ap, start=st, stop=False)
                        R.op("t", "matmul", PS["lg"][0:9, i * 128:(i + 1) * 128],
                             wr9_lo[:, kb * 9:(kb + 1) * 9], hi_ap, start=False, stop=False)
                        R.op("t", "matmul", PS["lg"][0:9, i * 128:(i + 1) * 128],
                             wr9_hi[:, kb * 9:(kb + 1) * 9], loT[:, q * 128:(q + 1) * 128],
                             start=False, stop=last)
                    R.mark("t", sT)
                    G["pstr_t"][pb] = sT.n
                    G["lo_t"] = sT.n

    G["xt_guard0"] = G["xt_guard1"] = 0

    def compute_z1():
        R.op("v", "memset", z1b[0:32, :], 1.0)
        R.mark("v", sV)
        v_ones = sV.n
        R.op("t", "wait_ge", cRT.h, wr9_ld)
        for kb in range(KB):
            for n in range(4):
                R.op("t", "matmul", PS["lg"][0:16, n * 512:(n + 1) * 512],
                     wgk1_b[:, kb * 16:(kb + 1) * 16],
                     xnT[:, kb * T + n * 512: kb * T + (n + 1) * 512],
                     start=(kb == 0), stop=(kb == KB - 1))
        R.mark("t", sT)
        t_z1 = sT.n
        R.op("v", "wait_ge", sT.h, t_z1)
        R.op("v", "wait_ge", sV.h, v_ones)
        R.op("v", "tensor_copy", z1b[0:16, :], PS["lg"][0:16, :])
        R.mark("v", sV)
        return sV.n

    def load_whead(col_src):
        sl = G["wslot"] % 4
        G["wslot"] += 1
        if G["wring_tok"][sl]:
            R.op("g", "wait_ge", sT.h, G["wring_tok"][sl])
        R.op("g", "dma_start", wring[sl][:, :].rearrange("p (kb m) -> p kb m", kb=KB),
             col_src).then_inc(cWR[sl].bump(16), 16)
        return sl, cWR[sl].n

    def load_wgk2(h):
        if G["wgk2_tok"]:
            R.op("g", "wait_ge", sT.h, G["wgk2_tok"])
        R.op("g", "dma_start", wgk2_b[0:16, :],
             wgk2_h[:, h * DH:(h + 1) * DH]).then_inc(cGK.bump(16), 16)
        R.op("g", "dma_start", wgk2_b[16:17, :],
             bgk2_h.rearrange("(o d) -> o d", o=1)[:, h * DH:(h + 1) * DH]).then_inc(cGK.bump(16), 16)
        G["wgk2_ld"] = cGK.n

    def proj_wait(pb):
        if G["proj_v"][pb]:
            R.op("t", "wait_ge", sV.h, G["proj_v"][pb])
        if G["proj_s"][pb]:
            R.op("t", "wait_ge", sS.h, G["proj_s"][pb])

    def mm_proj(slot, w_ld, out_T, mask=False):
        for half in range(2):
            pb = half
            R.op("t", "wait_ge", cWR[slot].h, w_ld)
            proj_wait(pb)
            for kb in range(KB):
                for n in range(2):
                    off = kb * T + half * 1024 + n * 512
                    R.op("t", "matmul", PS["proj"][pb][:, n * 512:(n + 1) * 512],
                         wring[slot][:, kb * 128:(kb + 1) * 128],
                         xnT[:, off:off + 512], start=(kb == 0), stop=(kb == KB - 1))
            R.mark("t", sT)
            t_mm = sT.n
            G["wring_tok"][slot] = t_mm
            R.op("v", "wait_ge", sT.h, t_mm)
            if mask:
                R.op("v", "tensor_scalar_mul", out_T[:, half * 1024:(half + 1) * 1024],
                     PS["proj"][pb][:, :], cols[:, CZ:CZ + 1])
            else:
                R.op("v", "tensor_copy", out_T[:, half * 1024:(half + 1) * 1024],
                     PS["proj"][pb][:, :])
            R.mark("v", sV)
            G["proj_v"][pb] = sV.n

    def mm_proj_v(slot, w_ld):
        for half in range(2):
            pb = half
            R.op("t", "wait_ge", cWR[slot].h, w_ld)
            proj_wait(pb)
            for kb in range(KB):
                for n in range(2):
                    off = kb * T + half * 1024 + n * 512
                    R.op("t", "matmul", PS["proj"][pb][:, n * 512:(n + 1) * 512],
                         wring[slot][:, kb * 128:(kb + 1) * 128],
                         xnT[:, off:off + 512], start=(kb == 0), stop=(kb == KB - 1))
            R.mark("t", sT)
            t_mm = sT.n
            G["wring_tok"][slot] = t_mm
            for n in range(2):
                R.op("v", "wait_ge", sT.h, t_mm)
                if G["vt_t"]:
                    R.op("v", "wait_ge", sT.h, G["vt_t"])
                R.op("v", "tensor_copy", tmpv[:, :], PS["proj"][pb][:, n * 512:(n + 1) * 512])
                R.mark("v", sV)
                v_tm = sV.n
                G["proj_v"][pb] = v_tm
                R.op("t", "wait_ge", sV.h, v_tm)
                if G["vt"]:
                    R.op("t", "wait_ge", sV.h, G["vt"])
                for q in range(4):
                    R.op("t", "transpose", PS["vt"][:, q * 128:(q + 1) * 128],
                         tmpv[:, q * 128:(q + 1) * 128], eye_bs[:, :])
                R.mark("t", sT)
                G["vt_t"] = sT.n
                R.op("v", "wait_ge", sT.h, sT.n)
                base = half * 1024 + n * 512
                R.op("v", "tensor_copy", vnat[:, base:base + 512], PS["vt"][:, :])
                R.mark("v", sV)
                G["vt"] = sV.n

    def mm_proj_gate(slot, w_ld):
        for half in range(2):
            pb = half
            R.op("t", "wait_ge", cWR[slot].h, w_ld)
            proj_wait(pb)
            for kb in range(KB):
                for n in range(2):
                    off = kb * T + half * 1024 + n * 512
                    R.op("t", "matmul", PS["proj"][pb][:, n * 512:(n + 1) * 512],
                         wring[slot][:, kb * 128:(kb + 1) * 128],
                         xnT[:, off:off + 512], start=(kb == 0), stop=(kb == KB - 1))
            R.mark("t", sT)
            t_mm = sT.n
            G["wring_tok"][slot] = t_mm
            R.op("s", "wait_ge", sT.h, t_mm)
            if G.get("sgT_guard"):
                R.op("s", "wait_ge", sV.h, G["sgT_guard"])
            R.op("s", "activation", sgT[:, half * 1024:(half + 1) * 1024],
                 PS["proj"][pb][:, :], AF.Silu)
            R.mark("s", sS)
            G["proj_s"][pb] = sS.n

    def mm_z2_gk(v_z1):
        for half in range(2):
            pb = half
            proj_wait(pb)
            R.op("t", "wait_ge", sV.h, v_z1)
            R.op("t", "wait_ge", cGK.h, G["wgk2_ld"])
            for n in range(2):
                off = half * 1024 + n * 512
                R.op("t", "matmul", PS["proj"][pb][:, n * 512:(n + 1) * 512],
                     wgk2_b[0:17, :], z1b[0:17, off:off + 512], start=True, stop=True)
            R.mark("t", sT)
            t_z2 = sT.n
            G["wgk2_tok"] = t_z2
            phase_done_cb[0]("q0" if half == 0 else "q0b")
            for n in range(4):
                qoff = half * 1024 + n * 256
                zsrc = PS["proj"][pb][:, n * 256:(n + 1) * 256]
                R.op("s", "wait_ge", sT.h, t_z2)
                if n or half:
                    R.op("s", "wait_ge", sV.h, sV.n)
                R.op("s", "activation", tmpA[:, :], zsrc, AF.Abs)
                s_abs = R.mark("s", sS)
                R.op("s", "activation", tmpB[:, :], tmpA[:, :], AF.Exp, scale=-1.0)
                R.op("s", "activation", tmpA[:, :], tmpB[:, :], AF.Ln,
                     bias=cols[:, CONE:CONE + 1])
                s_ln = R.mark("s", sS)
                R.op("v", "wait_ge", sS.h, s_abs)
                R.op("v", "tensor_scalar_min", tmpC[:, :], zsrc, 0.0)
                R.op("v", "wait_ge", sS.h, s_ln)
                R.op("v", "tensor_tensor", gku[:, qoff:qoff + 256], tmpC[:, :],
                     tmpA[:, :], ALU.subtract)
                R.mark("v", sV)
            G["proj_v"][pb] = sV.n
            G["proj_s"][pb] = sS.n
        phase_done_cb[0]("g1")
        if G["head_s_guard"]:
            R.op("v", "wait_ge", sS.h, G["head_s_guard"])
        for s4 in range(4):
            init = 0.0 if s4 == 0 else Gg[:, s4 * 512 - 1: s4 * 512]
            R.op("v", "tensor_tensor_scan", Gg[:, s4 * 512:(s4 + 1) * 512],
                 gku[:, s4 * 512:(s4 + 1) * 512], zeros_s[:, :], init, ALU.add, ALU.add)
        phase_done_cb[0]("g2")
        R.op("v", "tensor_copy", ccol[:, CEND * 16:CEND * 16 + 16], Gg[:, 127::128])
        R.op("v", "memset", ccol[:, CPRV * 16:CPRV * 16 + 1], 0.0)
        R.op("v", "tensor_copy", ccol[:, CPRV * 16 + 1:CPRV * 16 + 16],
             Gg[:, 127:T - 128:128])
        R.mark("v", sV)
        phase_done_cb[0]("g3")
        return sV.n

    try:
        psA = ExitStack()
        PS["tr"] = [psA.enter_context(nc.psum_tensor("ps_tr0", [128, 512], F32)),
                    psA.enter_context(nc.psum_tensor("ps_tr1", [128, 512], F32))]
        PS["lg"] = psA.enter_context(nc.psum_tensor("ps_lg", [16, 2048], F32))
        R.op("t", "wait_ge", cRT.h, wr9_ld)
        R.op("t", "wait_ge", cC.h, consts_ld)
        R.op("t", "transpose", PS["tr"][0][:, 0:16], wsgT[:, :], eye_fs[0:16, 0:16])
        t_wsg = R.mark("t", sT)
        R.op("v", "wait_ge", sT.h, t_wsg)
        R.op("v", "wait_ge", cRT.h, wr9_ld)
        R.op("v", "tensor_copy",
             wr9_f[:, :].rearrange("p (kb e) -> p kb e", kb=KB)[:, :, 0:8],
             wrf[:, :].rearrange("p (kb e) -> p kb e", kb=KB))
        R.op("v", "tensor_copy",
             wr9_f[:, :].rearrange("p (kb e) -> p kb e", kb=KB)[:, :, 8:9],
             PS["tr"][0][:, 0:16].rearrange("p (kb o) -> p kb o", o=1))
        R.op("v", "tensor_copy", wr9_hi[:, :], wr9_f[:, :])
        R.op("v", "tensor_tensor", wr9_lo[:, :], wr9_f[:, :], wr9_hi[:, :], ALU.subtract)
        v_wr9 = R.mark("v", sV)
        G["pstr_v"][0] = v_wr9
        ln_phase(xpre, False)
        barrier()
        v_z1 = compute_z1()
        barrier()
        phase_done("lnpre")
        psA.close()

        psB = ExitStack()
        PS["proj"] = [psB.enter_context(nc.psum_tensor("ps_pA", [128, 1024], F32)),
                      psB.enter_context(nc.psum_tensor("ps_pB", [128, 1024], F32))]
        PS["vt"] = psB.enter_context(nc.psum_tensor("ps_vt", [128, 512], BF16))
        PS["scb"] = psB.enter_context(nc.psum_tensor("ps_scb", [128, 1024], BF16))

        for h in range(H):
            slk, ldk = load_whead(wk_h.rearrange("(kb p) m -> p kb m", p=128)[:, :, h * DH:(h + 1) * DH])
            slv, ldv = load_whead(wv_h.rearrange("(kb p) m -> p kb m", p=128)[:, :, h * DH:(h + 1) * DH])
            load_wgk2(h)
            mm_proj(slk, ldk, kT, mask=True)
            if h == 0:
                barrier()
                phase_done("p_projk")
            mm_proj_v(slv, ldv)
            if h == 0:
                barrier()
                phase_done("p_projv")
            v_gk = mm_z2_gk(v_z1)
            if h == 0:
                barrier()
                phase_done("p_gk")
            R.op("v", "tensor_scalar_mul", ccol[:, CB2 * 16:CB2 * 16 + 16],
                 ccol[:, CEND * 16:CEND * 16 + 16], 1.0 / 16.0)
            R.op("v", "tensor_tensor", ccol[:, CTMP * 16:CTMP * 16 + 16],
                 ccol[:, CEND * 16:CEND * 16 + 16], ccol[:, CPRV * 16:CPRV * 16 + 16],
                 ALU.subtract)
            R.op("v", "memset", Sf[:, :], 0.0)
            v_cols = R.mark("v", sV)
            R.op("s", "wait_ge", sV.h, v_cols)
            R.op("s", "activation", ccol[:, CDEC * 16:CDEC * 16 + 16],
                 ccol[:, CTMP * 16:CTMP * 16 + 16], AF.Exp, scale=1.0 / 16.0)
            if G["pkw_v"]:
                R.op("s", "wait_ge", sV.h, G["pkw_v"])
            for j in range(NCH):
                R.op("s", "activation", E2all[:, j * 128:(j + 1) * 128],
                     Gg[:, j * 128:(j + 1) * 128], AF.Exp,
                     scale=-1.0 / 16.0, bias=ccol[:, CB2 * 16 + j:CB2 * 16 + j + 1])
            s_e4 = R.mark("s", sS)
            G["head_s_guard"] = s_e4
            R.op("v", "wait_ge", sS.h, s_e4)
            if G["pkw_pe"]:
                R.op("v", "wait_ge", sT.h, G["pkw_pe"])
            R.op("v", "tensor_tensor", kwall[:, :], kT[:, :], E2all[:, :], ALU.mult)
            v_kw = R.mark("v", sV)
            G["pkw_v"] = v_kw
            for half in range(2):
                R.op("t", "wait_ge", sV.h, v_kw)
                if G["scb"]:
                    R.op("t", "wait_ge", sV.h, G["scb"])
                for jj in range(8):
                    j = half * 8 + jj
                    R.op("t", "transpose", PS["scb"][:, jj * 128:(jj + 1) * 128],
                         kwall[:, j * 128:(j + 1) * 128], eye_bs[:, :])
                t_kwt = R.mark("t", sT)
                R.op("v", "wait_ge", sT.h, t_kwt)
                R.op("v", "tensor_copy", kwnall[:, half * 1024:(half + 1) * 1024],
                     PS["scb"][:, :])
                G["scb"] = R.mark("v", sV)
            v_kwn = G["scb"]
            R.op("t", "wait_ge", sV.h, v_kwn)
            proj_wait(0)
            proj_wait(1)
            for j in range(NCH):
                R.op("t", "matmul", PS["proj"][j // 8][:, (j % 8) * 128:(j % 8 + 1) * 128],
                     kwnall[:, j * 128:(j + 1) * 128], vnat[:, j * 128:(j + 1) * 128],
                     start=True, stop=True)
            t_su = R.mark("t", sT)
            G["pkw_pe"] = t_su
            R.op("v", "wait_ge", sT.h, t_su)
            for j in range(NCH):
                R.op("v", "scalar_tensor_tensor", Sf[:, :], Sf[:, :],
                     ccol[:, CDEC * 16 + j:CDEC * 16 + j + 1],
                     PS["proj"][j // 8][:, (j % 8) * 128:(j % 8 + 1) * 128],
                     ALU.mult, ALU.add)
            R.op("v", "tensor_copy", S_in[:, h * DH:(h + 1) * DH], Sf[:, :])
            schain = R.mark("v", sV)
            G["proj_v"][0] = schain
            G["proj_v"][1] = schain
            if h == 0:
                barrier()
                phase_done("p_head0")

        barrier()
        phase_done("prefix")
        psB.close()

        psA2 = ExitStack()
        PS["tr"] = [psA2.enter_context(nc.psum_tensor("ps_tr0b", [128, 512], F32)),
                    psA2.enter_context(nc.psum_tensor("ps_tr1b", [128, 512], F32))]
        PS["lg"] = psA2.enter_context(nc.psum_tensor("ps_lgb", [16, 2048], F32))
        G["pstr_s"] = [0, 0]
        G["pstr_v"] = [0, 0]
        ln_phase(xseg, True)
        barrier()
        phase_done("lnseg")

        rsr = ExitStack()

        def sbrr(name, shape, dt):
            return rsr.enter_context(nc.sbuf_tensor(name, list(shape), dt, side="right"))

        lgT = sbrr("lgT", (16, NT * 128), F32)
        lg_n = sbrr("lg_n", (128, NT * 9), F32)
        es_n = sbrr("es_n", (128, NT * 9), F32)
        rank_n = sbrr("rank_n", (128, NT * 9), F32)
        rk_init = sbrr("rk_init", (128, NT * 9), F32)
        s1_n = sbrr("s1_n", (128, NT), F32)
        m1v = sbrr("m1v", (128, NT), F32)

        R.op("y", "dma_start", rk_init[:, :], rank0_c).then_inc(cRK.bump(16), 16)
        rk_ld = cRK.n
        R.op("v", "memset", lgT[0:16, :], 0.0)
        R.op("v", "tensor_copy", lgT[0:9, :], PS["lg"][0:9, :])
        R.mark("v", sV)
        v_lgT = sV.n
        for i in range(NT):
            R.op("t", "wait_ge", sV.h, v_lgT)
            if i:
                R.op("t", "wait_ge", sV.h, sV.n)
            R.op("t", "transpose", PS["tr"][0][:, 0:16], lgT[0:16, i * 128:(i + 1) * 128],
                 eye_fs[0:16, 0:16])
            R.mark("t", sT)
            R.op("v", "wait_ge", sT.h, sT.n)
            R.op("v", "tensor_copy", lg_n[:, i * 9:(i + 1) * 9], PS["tr"][0][:, 0:9])
            R.mark("v", sV)
        R.op("v", "tensor_copy", m1v[:, :], lg_n[:, 0::9])
        for e in range(1, 8):
            R.op("v", "tensor_tensor", m1v[:, :], m1v[:, :], lg_n[:, e::9], ALU.max)
        for i in range(NT):
            R.op("v", "tensor_scalar_sub", es_n[:, i * 9:i * 9 + 9],
                 lg_n[:, i * 9:i * 9 + 9], m1v[:, i:i + 1])
        R.mark("v", sV)
        v_sub = sV.n
        R.op("s", "wait_ge", sV.h, v_sub)
        R.op("s", "activation", es_n[:, :], es_n[:, :], AF.Exp)
        R.op("s", "activation", sgate_n[:, :], lg_n[:, 8::9], AF.Exp, scale=-1.0)
        R.mark("s", sS)
        s_es = sS.n
        R.op("v", "wait_ge", sS.h, s_es)
        R.op("v", "tensor_scalar_add", sgate_n[:, :], sgate_n[:, :], 1.0)
        R.op("v", "reciprocal", sgate_n[:, :], sgate_n[:, :])
        R.op("v", "tensor_copy", s1_n[:, :], es_n[:, 0::9])
        for e in range(1, 8):
            R.op("v", "tensor_tensor", s1_n[:, :], s1_n[:, :], es_n[:, e::9], ALU.add)
        R.op("v", "wait_ge", cRK.h, rk_ld)
        R.op("v", "tensor_copy", rank_n[:, :], rk_init[:, :])
        for a in range(8):
            for b in range(a + 1, 8):
                R.op("v", "tensor_tensor", m1v[:, :], es_n[:, b::9], es_n[:, a::9], ALU.is_gt)
                R.op("v", "tensor_tensor", rank_n[:, a::9], rank_n[:, a::9], m1v[:, :], ALU.add)
                R.op("v", "tensor_tensor", rank_n[:, b::9], rank_n[:, b::9], m1v[:, :],
                     ALU.subtract)
        R.op("v", "tensor_single_scalar", rank_n[:, :], rank_n[:, :], 2.0, ALU.is_lt)
        R.op("v", "tensor_tensor", es_n[:, :], es_n[:, :], rank_n[:, :], ALU.mult)
        R.op("v", "reciprocal", s1_n[:, :], s1_n[:, :])
        for e in range(8):
            R.op("v", "tensor_tensor", C_nat[:, e::8], es_n[:, e::9], s1_n[:, :], ALU.mult)
        R.mark("v", sV)
        barrier()
        phase_done("router")
        rsr.close()
        rsLN.close()

        v_z1 = compute_z1()
        barrier()
        psA2.close()

        psB2 = ExitStack()
        PS["proj"] = [psB2.enter_context(nc.psum_tensor("ps_pA2", [128, 1024], F32)),
                      psB2.enter_context(nc.psum_tensor("ps_pB2", [128, 1024], F32))]
        PS["vt"] = psB2.enter_context(nc.psum_tensor("ps_vt2", [128, 512], BF16))
        PS["at"] = [psB2.enter_context(nc.psum_tensor("ps_at0", [128, 512], F32)),
                    psB2.enter_context(nc.psum_tensor("ps_at1", [128, 512], F32))]
        PS["scb"] = psB2.enter_context(nc.psum_tensor("ps_scb2", [128, 1024], BF16))
        G["vt"] = G["scb"] = 0
        G["at_g"] = [0, 0]
        G["Eread"] = 0
        G["gate_pe"] = 0
        ogh_store = [0]
        rsAT = ExitStack()

        def sbat(name, shape, dt):
            return rsAT.enter_context(nc.sbuf_tensor(name, list(shape), dt, side="right"))

        qT = sbat("qT", (128, T), BF16)
        sgT = sbat("sgT", (128, T), BF16)
        E1all = sbat("E1all", (128, T), F32)
        qgall = sbat("qgall", (128, T), BF16)
        qhall = sbat("qhall", (128, T), BF16)
        khall = sbat("khall", (128, T), BF16)
        atball = sbat("atball", (128, T), BF16)
        SbfAll = sbat("SbfAll", (128, (NCH + 1) * 128), BF16)
        of32p = sbat("of32p", (128, 1024), F32)
        oscp = sbat("oscp", (128, 1024), BF16)
        ogh = sbat("ogh", (128, T), BF16)

        for h in range(H):
            slq, ldq = load_whead(wq_h.rearrange("(kb p) m -> p kb m", p=128)[:, :, h * DH:(h + 1) * DH])
            slk, ldk = load_whead(wk_h.rearrange("(kb p) m -> p kb m", p=128)[:, :, h * DH:(h + 1) * DH])
            slv, ldv = load_whead(wv_h.rearrange("(kb p) m -> p kb m", p=128)[:, :, h * DH:(h + 1) * DH])
            slg, ldg = load_whead(wgate_h.rearrange("(kb p) m -> p kb m", p=128)[:, :, h * DH:(h + 1) * DH])
            load_wgk2(h)
            mm_proj(slq, ldq, qT, mask=False)
            mm_proj(slk, ldk, kT, mask=False)
            mm_proj_v(slv, ldv)
            mm_proj_gate(slg, ldg)
            v_gk = mm_z2_gk(v_z1)
            R.op("v", "tensor_copy", ccol[:, CTMP * 16:CTMP * 16 + 16], Gg[:, 0::128])
            R.op("v", "tensor_scalar", ccol[:, CB1 * 16:CB1 * 16 + 16],
                 ccol[:, CPRV * 16:CPRV * 16 + 16], -1.0 / 16.0, LNSCALE, ALU.mult, ALU.add)
            R.op("v", "tensor_scalar_mul", ccol[:, CB2 * 16:CB2 * 16 + 16],
                 ccol[:, CTMP * 16:CTMP * 16 + 16], 1.0 / 16.0)
            R.op("v", "tensor_tensor", ccol[:, CC1 * 16:CC1 * 16 + 16],
                 ccol[:, CPRV * 16:CPRV * 16 + 16], ccol[:, CTMP * 16:CTMP * 16 + 16],
                 ALU.subtract)
            R.op("v", "tensor_tensor", ccol[:, CC2 * 16:CC2 * 16 + 16],
                 ccol[:, CEND * 16:CEND * 16 + 16], ccol[:, CTMP * 16:CTMP * 16 + 16],
                 ALU.subtract)
            R.op("v", "tensor_tensor", ccol[:, CDEC * 16:CDEC * 16 + 16],
                 ccol[:, CEND * 16:CEND * 16 + 16], ccol[:, CPRV * 16:CPRV * 16 + 16],
                 ALU.subtract)
            v_cols = R.mark("v", sV)
            R.op("s", "wait_ge", sV.h, v_cols)
            R.op("s", "activation", ccol[:, CC1 * 16:CC1 * 16 + 16],
                 ccol[:, CC1 * 16:CC1 * 16 + 16], AF.Exp, scale=1.0 / 16.0)
            R.op("s", "activation", ccol[:, CC2 * 16:CC2 * 16 + 16],
                 ccol[:, CC2 * 16:CC2 * 16 + 16], AF.Exp, scale=1.0 / 16.0)
            R.op("s", "activation", ccol[:, CDEC * 16:CDEC * 16 + 16],
                 ccol[:, CDEC * 16:CDEC * 16 + 16], AF.Exp, scale=1.0 / 16.0)
            s_cols = R.mark("s", sS)
            if G["Eread"]:
                R.op("s", "wait_ge", sV.h, G["Eread"])
            for j in range(NCH):
                R.op("s", "activation", E1all[:, j * 128:(j + 1) * 128],
                     Gg[:, j * 128:(j + 1) * 128], AF.Exp,
                     scale=1.0 / 16.0, bias=ccol[:, CB1 * 16 + j:CB1 * 16 + j + 1])
                R.op("s", "activation", E2all[:, j * 128:(j + 1) * 128],
                     Gg[:, j * 128:(j + 1) * 128], AF.Exp,
                     scale=-1.0 / 16.0, bias=ccol[:, CB2 * 16 + j:CB2 * 16 + j + 1])
            s_e = R.mark("s", sS)
            G["head_s_guard"] = s_e
            R.op("v", "wait_ge", sS.h, s_e)
            if G["gate_pe"]:
                R.op("v", "wait_ge", sT.h, G["gate_pe"])
            R.op("v", "tensor_tensor", qgall[:, :], qT[:, :], E1all[:, :], ALU.mult)
            R.op("v", "tensor_tensor", khall[:, :], kT[:, :], E2all[:, :], ALU.mult)
            for j in range(NCH):
                R.op("v", "tensor_scalar_mul", qhall[:, j * 128:(j + 1) * 128],
                     qgall[:, j * 128:(j + 1) * 128], ccol[:, CC1 * 16 + j:CC1 * 16 + j + 1])
                R.op("v", "tensor_scalar_mul", kwall[:, j * 128:(j + 1) * 128],
                     khall[:, j * 128:(j + 1) * 128], ccol[:, CC2 * 16 + j:CC2 * 16 + j + 1])
            v_gates = R.mark("v", sV)
            G["Eread"] = v_gates
            for b4 in range(4):
                pb = b4 % 2
                R.op("t", "wait_ge", sV.h, v_gates)
                if G["at_g"][pb]:
                    R.op("t", "wait_ge", sV.h, G["at_g"][pb])
                for jj in range(4):
                    j = b4 * 4 + jj
                    R.op("t", "matmul", PS["at"][pb][:, jj * 128:(jj + 1) * 128],
                         khall[:, j * 128:(j + 1) * 128], qhall[:, j * 128:(j + 1) * 128],
                         start=True, stop=True)
                t_at = R.mark("t", sT)
                R.op("v", "wait_ge", sT.h, t_at)
                R.op("v", "tensor_tensor", atball[:, b4 * 512:(b4 + 1) * 512],
                     PS["at"][pb][:, :], triu4_s[:, :], ALU.mult)
                G["at_g"][pb] = R.mark("v", sV)
            for half in range(2):
                R.op("t", "wait_ge", sV.h, v_gates)
                if G["scb"]:
                    R.op("t", "wait_ge", sV.h, G["scb"])
                for jj in range(8):
                    j = half * 8 + jj
                    R.op("t", "transpose", PS["scb"][:, jj * 128:(jj + 1) * 128],
                         kwall[:, j * 128:(j + 1) * 128], eye_bs[:, :])
                t_kwt = R.mark("t", sT)
                R.op("v", "wait_ge", sT.h, t_kwt)
                R.op("v", "tensor_copy", kwnall[:, half * 1024:(half + 1) * 1024],
                     PS["scb"][:, :])
                G["scb"] = R.mark("v", sV)
            v_kwn = G["scb"]
            R.op("t", "wait_ge", sV.h, v_kwn)
            proj_wait(0)
            proj_wait(1)
            for j in range(NCH):
                R.op("t", "matmul", PS["proj"][j // 8][:, (j % 8) * 128:(j % 8 + 1) * 128],
                     kwnall[:, j * 128:(j + 1) * 128], vnat[:, j * 128:(j + 1) * 128],
                     start=True, stop=True)
            t_su = R.mark("t", sT)
            G["gate_pe"] = t_su
            R.op("v", "wait_ge", sT.h, t_su)
            R.op("v", "wait_ge", sS.h, s_cols)
            R.op("v", "tensor_copy", Sf[:, :], S_in[:, h * DH:(h + 1) * DH])
            R.op("v", "tensor_copy", SbfAll[:, 0:128], Sf[:, :])
            chainmark = [R.mark("v", sV)]
            for j in range(NCH - 1):
                R.op("v", "scalar_tensor_tensor", Sf[:, :], Sf[:, :],
                     ccol[:, CDEC * 16 + j:CDEC * 16 + j + 1],
                     PS["proj"][j // 8][:, (j % 8) * 128:(j % 8 + 1) * 128],
                     ALU.mult, ALU.add)
                R.op("v", "tensor_copy", SbfAll[:, (j + 1) * 128:(j + 2) * 128], Sf[:, :])
                chainmark.append(R.mark("v", sV))
            G["proj_v"][0] = chainmark[-1]
            G["proj_v"][1] = chainmark[-1]
            ob = 0
            for b4 in range(4):
                pb = b4 % 2
                R.op("t", "wait_ge", sV.h, chainmark[min(b4 * 4 + 3, NCH - 1)])
                if G["at_g"][pb]:
                    R.op("t", "wait_ge", sV.h, G["at_g"][pb])
                for jj in range(4):
                    j = b4 * 4 + jj
                    R.op("t", "matmul", PS["at"][pb][:, jj * 128:(jj + 1) * 128],
                         atball[:, j * 128:(j + 1) * 128], vnat[:, j * 128:(j + 1) * 128],
                         start=True, stop=False)
                    R.op("t", "matmul", PS["at"][pb][:, jj * 128:(jj + 1) * 128],
                         qgall[:, j * 128:(j + 1) * 128], SbfAll[:, j * 128:(j + 1) * 128],
                         start=False, stop=True)
                t_o = R.mark("t", sT)
                R.op("v", "wait_ge", sT.h, t_o)
                if h == 0 and b4 < 2:
                    pass
                R.op("v", "tensor_copy", of32p[:, pb * 512:(pb + 1) * 512], PS["at"][pb][:, :])
                for jj in range(4):
                    j = b4 * 4 + jj
                    R.op("v", "scalar_tensor_tensor", tr128[:, :],
                         of32p[:, pb * 512 + jj * 128: pb * 512 + (jj + 1) * 128], 1.0,
                         of32p[:, pb * 512 + jj * 128: pb * 512 + (jj + 1) * 128],
                         ALU.mult, ALU.mult, accum_out=sscol[:, j:j + 1])
                v_ssq = R.mark("v", sV)
                G["at_g"][pb] = v_ssq
                R.op("s", "wait_ge", sV.h, v_ssq)
                R.op("s", "activation", rstdall[:, b4 * 4:b4 * 4 + 4],
                     sscol[:, b4 * 4:b4 * 4 + 4], AF.Ln, bias=cols[:, CE128:CE128 + 1])
                R.op("s", "activation", rstdall[:, b4 * 4:b4 * 4 + 4],
                     rstdall[:, b4 * 4:b4 * 4 + 4], AF.Exp, scale=-0.5)
                s_rs = R.mark("s", sS)
                R.op("v", "wait_ge", sS.h, s_rs)
                for jj in range(4):
                    j = b4 * 4 + jj
                    R.op("v", "tensor_scalar_mul",
                         oscp[:, pb * 512 + jj * 128: pb * 512 + (jj + 1) * 128],
                         of32p[:, pb * 512 + jj * 128: pb * 512 + (jj + 1) * 128],
                         rstdall[:, j:j + 1])
                v_osc = R.mark("v", sV)
                R.op("t", "wait_ge", sV.h, v_osc)
                if G["scb"]:
                    R.op("t", "wait_ge", sV.h, G["scb"])
                for jj in range(4):
                    R.op("t", "transpose",
                         PS["scb"][:, pb * 512 + jj * 128: pb * 512 + (jj + 1) * 128],
                         oscp[:, pb * 512 + jj * 128: pb * 512 + (jj + 1) * 128],
                         eye_bs[:, :])
                t_ot = R.mark("t", sT)
                R.op("v", "wait_ge", sT.h, t_ot)
                if ogh_store[0]:
                    R.op("v", "wait_ge", cOG[0].h, ogh_store[0])
                R.op("v", "scalar_tensor_tensor", ogh[:, b4 * 512:(b4 + 1) * 512],
                     PS["scb"][:, pb * 512:(pb + 1) * 512], gnorm_c[:, 0:1],
                     sgT[:, b4 * 512:(b4 + 1) * 512], ALU.mult, ALU.mult)
                G["scb"] = R.mark("v", sV)
            R.op("y", "wait_ge", sV.h, sV.n)
            R.op("y", "dma_start", og_hbm[h * DH:(h + 1) * DH, :],
                 ogh[:, :]).then_inc(cOG[0].bump(16), 16)
            ogh_store[0] = cOG[0].n

        barrier()
        phase_done("attn")
        psB2.close()
        rsAT.close()
        rs1.close()
        rs0.close()

        psW = ExitStack()
        PS["proj"] = [psW.enter_context(nc.psum_tensor("ps_woA", [128, 1024], F32)),
                      psW.enter_context(nc.psum_tensor("ps_woB", [128, 1024], F32))]
        rsW = ExitStack()

        def sbrw(name, shape, dt):
            return rsW.enter_context(nc.sbuf_tensor(name, list(shape), dt, side="right"))

        wo_b = sbrw("wo_b", (128, KB * D), BF16)
        ogsl = [sbrw("ogsl0", (128, H * 128), BF16), sbrw("ogsl1", (128, H * 128), BF16)]
        xtw = [sbrw("xtw0", (128, D), F32), sbrw("xtw1", (128, D), F32)]
        o1sb = [sbrw("o1sb0", (128, 1024), F32), sbrw("o1sb1", (128, 1024), F32)]
        R.op("g", "dma_start", wo_b[:, :].rearrange("p (kb n) -> p kb n", kb=KB),
             wo_h.rearrange("(kb p) n -> p kb n", p=128)).then_inc(cWO.bump(16), 16)
        wo_ld = cWO.n
        G["proj_v"] = [0, 0]
        G["proj_s"] = [0, 0]
        ogsl_tok = [0, 0]
        xtw_tok = [0, 0]
        o1_store = [0, 0]
        for tb in range(NT):
            buf = tb % 2
            if ogsl_tok[buf]:
                R.op("y", "wait_ge", sT.h, ogsl_tok[buf])
            R.op("y", "dma_start", ogsl[buf][:, :].rearrange("p (hh m) -> p hh m", hh=H),
                 og_hbm[:, tb * 128:(tb + 1) * 128].rearrange("(hh p) m -> p hh m", p=128)
                 ).then_inc(cOS[buf].bump(16), 16)
            og_ld = cOS[buf].n
            if xtw_tok[buf]:
                R.op("y", "wait_ge", sV.h, xtw_tok[buf])
            R.op("y", "dma_start", xtw[buf][:, :],
                 xseg[tb * 128:(tb + 1) * 128, :]).then_inc(cXW[buf].bump(16), 16)
            x_ld = cXW[buf].n
            for half in range(2):
                pb = half
                R.op("t", "wait_ge", cWO.h, wo_ld)
                R.op("t", "wait_ge", cOS[buf].h, og_ld)
                proj_wait(pb)
                for hh in range(H):
                    for n in range(2):
                        woff = hh * D + half * 1024 + n * 512
                        R.op("t", "matmul", PS["proj"][pb][:, n * 512:(n + 1) * 512],
                             ogsl[buf][:, hh * 128:(hh + 1) * 128],
                             wo_b[:, woff:woff + 512], start=(hh == 0), stop=(hh == H - 1))
                R.mark("t", sT)
                t_wo = sT.n
                if half == 1:
                    ogsl_tok[buf] = t_wo
                R.op("v", "wait_ge", sT.h, t_wo)
                R.op("v", "wait_ge", cXW[buf].h, x_ld)
                if o1_store[half]:
                    R.op("v", "wait_ge", cO1[half].h, o1_store[half])
                R.op("v", "tensor_tensor", o1sb[half][:, :], PS["proj"][pb][:, :],
                     xtw[buf][:, half * 1024:(half + 1) * 1024], ALU.add)
                R.mark("v", sV)
                G["proj_v"][pb] = sV.n
                xtw_tok[buf] = sV.n
                R.op("y", "wait_ge", sV.h, sV.n)
                R.op("y", "dma_start",
                     out1_hbm[tb * 128:(tb + 1) * 128, half * 1024:(half + 1) * 1024],
                     o1sb[half][:, :]).then_inc(cO1[half].bump(16), 16)
                o1_store[half] = cO1[half].n

        barrier()
        phase_done("wo")
        psW.close()
        rsW.close()

        psF = ExitStack()
        PS["proj"] = [psF.enter_context(nc.psum_tensor("ps_gA", [128, 1024], F32)),
                      psF.enter_context(nc.psum_tensor("ps_gB", [128, 1024], F32))]
        PS["u"] = [psF.enter_context(nc.psum_tensor("ps_uA", [128, 1024], F32)),
                   psF.enter_context(nc.psum_tensor("ps_uB", [128, 1024], F32))]
        rsF = ExitStack()

        def sbrf(name, shape, dt):
            return rsF.enter_context(nc.sbuf_tensor(name, list(shape), dt, side="right"))

        wgu = [sbrf(f"wgu{i}", (128, KB * 128), BF16) for i in range(4)]
        sgil = sbrf("sgil", (128, 1024), F32)
        hsb = [sbrf("hsb0", (128, T), BF16), sbrf("hsb1", (128, T), BF16)]
        wgu_tok = [0] * 4
        hsb_store = [0, 0]
        G["proj_v"] = [0, 0]
        G["proj_s"] = [0, 0]
        u_guard = [0, 0]
        sgil_guard = 0

        def gu_src(e, fb, which):
            if e < 8:
                t = ewg_h if which == 0 else ewu_h
                return t[e, :, :].rearrange("(kb p) m -> p kb m", p=128)[:, :, fb * 128:(fb + 1) * 128]
            t = sgw_h if which == 0 else suw_h
            off = (e - 8) * F + fb * 128
            return t.rearrange("(kb p) m -> p kb m", p=128)[:, :, off:off + 128]

        idx = 0
        for e in range(NE):
            for fb in range(FB):
                sg_slot = (2 * idx) % 4
                su_slot = (2 * idx + 1) % 4
                if wgu_tok[sg_slot]:
                    R.op("g", "wait_ge", sT.h, wgu_tok[sg_slot])
                R.op("g", "dma_start", wgu[sg_slot][:, :].rearrange("p (kb m) -> p kb m", kb=KB),
                     gu_src(e, fb, 0)).then_inc(cGU[sg_slot].bump(16), 16)
                ld_g = cGU[sg_slot].n
                if wgu_tok[su_slot]:
                    R.op("g", "wait_ge", sT.h, wgu_tok[su_slot])
                R.op("g", "dma_start", wgu[su_slot][:, :].rearrange("p (kb m) -> p kb m", kb=KB),
                     gu_src(e, fb, 1)).then_inc(cGU[su_slot].bump(16), 16)
                ld_u = cGU[su_slot].n
                hb = idx % 2
                for half in range(2):
                    R.op("t", "wait_ge", cGU[sg_slot].h, ld_g)
                    if G["proj_s"][half]:
                        R.op("t", "wait_ge", sS.h, G["proj_s"][half])
                    for kb in range(KB):
                        for n in range(2):
                            off = kb * T + half * 1024 + n * 512
                            R.op("t", "matmul", PS["proj"][half][:, n * 512:(n + 1) * 512],
                                 wgu[sg_slot][:, kb * 128:(kb + 1) * 128],
                                 xnT[:, off:off + 512], start=(kb == 0), stop=(kb == KB - 1))
                    R.mark("t", sT)
                    t_g = sT.n
                    R.op("t", "wait_ge", cGU[su_slot].h, ld_u)
                    if u_guard[half]:
                        R.op("t", "wait_ge", sV.h, u_guard[half])
                    for kb in range(KB):
                        for n in range(2):
                            off = kb * T + half * 1024 + n * 512
                            R.op("t", "matmul", PS["u"][half][:, n * 512:(n + 1) * 512],
                                 wgu[su_slot][:, kb * 128:(kb + 1) * 128],
                                 xnT[:, off:off + 512], start=(kb == 0), stop=(kb == KB - 1))
                    R.mark("t", sT)
                    t_u = sT.n
                    if half == 1:
                        wgu_tok[sg_slot] = t_g
                        wgu_tok[su_slot] = t_u
                    R.op("s", "wait_ge", sT.h, t_g)
                    if sgil_guard:
                        R.op("s", "wait_ge", sV.h, sgil_guard)
                    R.op("s", "activation", sgil[:, :], PS["proj"][half][:, :], AF.Silu)
                    R.mark("s", sS)
                    s_sil = sS.n
                    G["proj_s"][half] = s_sil
                    R.op("v", "wait_ge", sS.h, s_sil)
                    R.op("v", "wait_ge", sT.h, t_u)
                    if hsb_store[hb]:
                        R.op("v", "wait_ge", cH[hb].h, hsb_store[hb])
                    R.op("v", "tensor_tensor", hsb[hb][:, half * 1024:(half + 1) * 1024],
                         PS["u"][half][:, :], sgil[:, :], ALU.mult)
                    R.mark("v", sV)
                    u_guard[half] = sV.n
                    sgil_guard = sV.n
                R.op("y", "wait_ge", sV.h, sV.n)
                R.op("y", "dma_start", h_hbm[e * F + fb * 128: e * F + (fb + 1) * 128, :],
                     hsb[hb][:, :]).then_inc(cH[hb].bump(16), 16)
                hsb_store[hb] = cH[hb].n
                idx += 1

        barrier()
        phase_done("ffna")
        psF.close()
        rsF.close()
        xnT_es.close()

        psM = ExitStack()
        ps_moe = [psM.enter_context(nc.psum_tensor("ps_m0", [128, 512], F32)),
                  psM.enter_context(nc.psum_tensor("ps_m1", [128, 512], F32))]
        rsB = ExitStack()

        def sbrb(name, shape, dt):
            return rsB.enter_context(nc.sbuf_tensor(name, list(shape), dt, side="right"))

        acc = sbrb("acc", (128, 8 * D), F32)
        wd_b = [sbrb("wd0", (128, FB * D), BF16), sbrb("wd1", (128, FB * D), BF16)]
        hsl = [sbrb("hsl0", (128, FB * 128), BF16), sbrb("hsl1", (128, FB * 128), BF16)]

        def wd_src(e):
            if e < 8:
                return ewd_h[e, :, :].rearrange("(fb p) n -> p fb n", p=128)
            return sdw_h[(e - 8) * F:(e - 7) * F, :].rearrange("(fb p) n -> p fb n", p=128)

        wd_tok = [0, 0]
        hsl_tok = [0, 0]
        moe_guard = [0, 0]
        acc_store = 0
        for sub in range(2):
            R.op("y", "wait_ge", sV.h, sV.n)
            if acc_store:
                R.op("y", "wait_ge", cY.h, acc_store)
            R.op("y", "dma_start", acc[:, :].rearrange("p (tb d) -> p tb d", tb=8),
                 out1_hbm[sub * 1024:(sub + 1) * 1024, :].rearrange("(tb p) d -> p tb d", p=128)
                 ).then_inc(cACC.bump(16), 16)
            acc_ld = cACC.n
            for e in range(NE):
                sl = e % 2
                if wd_tok[sl]:
                    R.op("g", "wait_ge", sT.h, wd_tok[sl])
                R.op("g", "dma_start", wd_b[sl][:, :].rearrange("p (fb n) -> p fb n", fb=FB),
                     wd_src(e)).then_inc(cWD[sl].bump(16), 16)
                ld_wd = cWD[sl].n
                for tb8 in range(8):
                    tb = sub * 8 + tb8
                    hb = tb8 % 2
                    if hsl_tok[hb]:
                        R.op("y", "wait_ge", sT.h, hsl_tok[hb])
                    R.op("y", "dma_start", hsl[hb][:, :].rearrange("p (fb m) -> p fb m", fb=FB),
                         h_hbm[e * F:(e + 1) * F, tb * 128:(tb + 1) * 128].rearrange(
                             "(fb p) m -> p fb m", p=128)).then_inc(cHL[hb].bump(16), 16)
                    ld_h = cHL[hb].n
                    for dc in range(4):
                        pb = (tb8 * 4 + dc) % 2
                        R.op("t", "wait_ge", cWD[sl].h, ld_wd)
                        R.op("t", "wait_ge", cHL[hb].h, ld_h)
                        if moe_guard[pb]:
                            R.op("t", "wait_ge", sV.h, moe_guard[pb])
                        for fb in range(FB):
                            R.op("t", "matmul", ps_moe[pb][:, :],
                                 hsl[hb][:, fb * 128:(fb + 1) * 128],
                                 wd_b[sl][:, fb * D + dc * 512: fb * D + (dc + 1) * 512],
                                 start=(fb == 0), stop=(fb == FB - 1))
                        R.mark("t", sT)
                        t_moe = sT.n
                        if dc == 3:
                            hsl_tok[hb] = t_moe
                            if tb8 == 7:
                                wd_tok[sl] = t_moe
                        wc = C_nat[:, tb * 8 + e: tb * 8 + e + 1] if e < 8 else sgate_n[:, tb:tb + 1]
                        R.op("v", "wait_ge", sT.h, t_moe)
                        if e == 0 and dc == 0 and tb8 == 0:
                            R.op("v", "wait_ge", cACC.h, acc_ld)
                        R.op("v", "scalar_tensor_tensor",
                             acc[:, tb8 * D + dc * 512: tb8 * D + (dc + 1) * 512],
                             ps_moe[pb][:, :], wc,
                             acc[:, tb8 * D + dc * 512: tb8 * D + (dc + 1) * 512],
                             ALU.mult, ALU.add)
                        R.mark("v", sV)
                        moe_guard[pb] = sV.n
            R.op("y", "wait_ge", sV.h, sV.n)
            R.op("y", "dma_start",
                 y_out[sub * 1024:(sub + 1) * 1024, :].rearrange("(tb p) d -> p tb d", p=128),
                 acc[:, :].rearrange("p (tb d) -> p tb d", tb=8)).then_inc(cY.bump(16), 16)
            acc_store = cY.n

        R.op("y", "wait_ge", cY.h, cY.n)
    except _Done:
        R.op("y", "dma_start", y_out, xseg).then_inc(cY.bump(16), 16)
        R.op("y", "wait_ge", cY.h, cY.n)

    R.selfsem["v"] = sV.h
    R.selfsem["s"] = sS.h
    with nc.Block() as block:
        @block.tensor
        def _(t):
            R.replay("t", t)

        @block.vector
        def _(v):
            R.replay("v", v)

        @block.scalar
        def _(s):
            R.replay("s", s)

        @block.gpsimd
        def _(g):
            R.replay("g", g)

        @block.sync
        def _(yq):
            R.replay("y", yq)

    if phase_limit == "full":
        psM.close()
        rsB.close()
        es.close()
    return nc


_NC_CACHE = []


def kernel(**inputs):
    x = np.ascontiguousarray(np.asarray(inputs["L0_attn_x"], dtype=np.float32))
    B, L, _ = x.shape
    if not _NC_CACHE:
        _NC_CACHE.append(build_nc())
    nc = _NC_CACHE[0]
    base = {
        "wq": np.ascontiguousarray(inputs["L0_attn_Wq"], np.float32),
        "wk": np.ascontiguousarray(inputs["L0_attn_Wk"], np.float32),
        "wv": np.ascontiguousarray(inputs["L0_attn_Wv"], np.float32),
        "wgk1": np.ascontiguousarray(inputs["L0_attn_Wgk1"], np.float32),
        "wgk2": np.ascontiguousarray(inputs["L0_attn_Wgk2"], np.float32),
        "bgk2": np.ascontiguousarray(inputs["L0_attn_bgk2"], np.float32),
        "wgate": np.ascontiguousarray(inputs["L0_attn_Wg"], np.float32),
        "gnw": np.ascontiguousarray(inputs["L0_attn_gnorm_w"], np.float32),
        "wo": np.ascontiguousarray(inputs["L0_attn_Wo"], np.float32),
        "wr": np.ascontiguousarray(inputs["L0_ffn_Wr"], np.float32),
        "wsg": np.ascontiguousarray(inputs["L0_ffn_Wsg"], np.float32),
        "ewg": np.ascontiguousarray(inputs["L0_ffn_Wg"], np.float32),
        "ewu": np.ascontiguousarray(inputs["L0_ffn_Wu"], np.float32),
        "ewd": np.ascontiguousarray(inputs["L0_ffn_Wd"], np.float32),
        "sgw": np.ascontiguousarray(inputs["L0_ffn_Sg"], np.float32),
        "suw": np.ascontiguousarray(inputs["L0_ffn_Su"], np.float32),
        "sdw": np.ascontiguousarray(inputs["L0_ffn_Sd"], np.float32),
    }
    in_maps = []
    for c in range(8):
        b, s = c // 2, c % 2
        m = dict(base)
        m["xseg"] = np.ascontiguousarray(x[b, s * T:(s + 1) * T, :])
        m["xpre"] = np.ascontiguousarray(x[b, 0:T, :])
        m["zcol"] = np.full((128, 1), 1.0 if s == 1 else 0.0, np.float32)
        in_maps.append(m)
    res = run_bass_kernel_spmd(nc, in_maps, core_ids=list(range(8)))
    out = np.empty((B, L, D), np.float32)
    for c in range(8):
        b, s = c // 2, c % 2
        out[b, s * T:(s + 1) * T, :] = res.results[c]["y"]
    return out


# version 03
# speedup vs baseline: 1.1778x; 1.0647x over previous
import numpy as np
from contextlib import ExitStack

import ml_dtypes
import concourse.bass as bass
import concourse.mybir as mybir
from concourse.bass_utils import run_bass_kernel_spmd

F32 = mybir.dt.float32
BF16 = mybir.dt.bfloat16
AF = mybir.ActivationFunctionType
ALU = mybir.AluOpType

T = 2048
D = 2048
H = 16
DH = 128
CH = 128
NCH = T // CH
KB = D // 128
NT = T // 128
F = 1408
FB = F // 128
NE = 12
FS = 5632
EPS = 1e-5
LNSCALE = float(np.log(np.float32(128.0) ** np.float32(-0.5)))
SQ2048 = float(np.sqrt(np.float32(2048.0)))
SQ128 = float(np.sqrt(np.float32(128.0)))

ENGS = ("t", "v", "s", "g", "y")


class _Inst:
    __slots__ = ("attr", "args", "kw", "incs")

    def __init__(self, attr, args, kw):
        self.attr, self.args, self.kw = attr, args, kw
        self.incs = []

    def then_inc(self, sem, v=1):
        self.incs.append((sem, v))
        return self


class Rec:

    SELF = ("v", "s")

    def __init__(self):
        self.ev = {e: [] for e in ENGS}
        self.cnt = {e: 0 for e in self.SELF}
        self.selfsem = {}

    def op(self, eng, attr, *args, **kw):
        inst = _Inst(attr, args, kw)
        self.ev[eng].append(inst)
        if eng in self.SELF and attr != "wait_ge":
            self.cnt[eng] += 1
        return inst

    def mark(self, eng, sem):
        if eng in self.SELF:
            return self.cnt[eng]
        for inst in reversed(self.ev[eng]):
            if inst.attr != "wait_ge":
                for (h, _v) in inst.incs:
                    if h is sem.h:
                        return sem.n
                sem.bump()
                inst.incs.append((sem.h, 1))
                return sem.n
        raise AssertionError(f"no real instruction to mark on {eng}")

    def replay(self, eng_name, eng_obj):
        if eng_name in self.SELF:
            sem = self.selfsem[eng_name]
            idx = 0
            for inst in self.ev[eng_name]:
                if inst.attr == "wait_ge":
                    getattr(eng_obj, inst.attr)(*inst.args, **inst.kw)
                    continue
                if idx > 0:
                    eng_obj.wait_ge(sem, idx)
                idx += 1
                real = getattr(eng_obj, inst.attr)(*inst.args, **inst.kw)
                real.then_inc(sem, 1)
                for (s, v) in inst.incs:
                    real.then_inc(s, v)
            return
        for inst in self.ev[eng_name]:
            real = getattr(eng_obj, inst.attr)(*inst.args, **inst.kw)
            for (sem, v) in inst.incs:
                real.then_inc(sem, v)


class Sem:
    def __init__(self, es, nc, name, rec=None, eng=None):
        self.h = es.enter_context(nc.semaphore(name))
        self._n = 0
        self._rec, self._eng = rec, eng

    @property
    def n(self):
        if self._rec is not None:
            return self._rec.cnt[self._eng]
        return self._n

    def bump(self, k=1):
        assert self._rec is None
        self._n += k
        return self.h


def build_nc(phase_limit="full"):
    nc = bass.Bass("TRN2", target_bir_lowering=False)

    class _Done(Exception):
        pass

    es = ExitStack()
    R = Rec()

    def din(name, shape, dt=F32):
        return nc.dram_tensor(name, list(shape), dt, kind="ExternalInput").ap()

    xseg = din("xseg", (T, D))
    xpre = din("xpre", (T, D))
    zcol_h = din("zcol", (128, 1))
    wq_h = din("wq", (D, D)); wk_h = din("wk", (D, D)); wv_h = din("wv", (D, D))
    wgk1_h = din("wgk1", (D, 16)); wgk2_h = din("wgk2", (16, D)); bgk2_h = din("bgk2", (D,))
    wgate_h = din("wgate", (D, D)); gnw_h = din("gnw", (DH,)); wo_h = din("wo", (D, D))
    wr_h = din("wr", (D, 8)); wsg_h = din("wsg", (D, 1))
    ewg_h = din("ewg", (8, D, F)); ewu_h = din("ewu", (8, D, F)); ewd_h = din("ewd", (8, F, D))
    sgw_h = din("sgw", (D, FS)); suw_h = din("suw", (D, FS)); sdw_h = din("sdw", (FS, D))

    y_out = nc.dram_tensor("y", [T, D], F32, kind="ExternalOutput").ap()
    h_hbm = nc.dram_tensor("h_scr", [NE * F, T], BF16, kind="Internal").ap()
    out1_hbm = nc.dram_tensor("out1_scr", [T, D], F32, kind="Internal").ap()
    og_hbm = nc.dram_tensor("og_scr", [H * DH, T], BF16, kind="Internal").ap()

    eye_f_c = nc.inline_tensor(np.eye(128, dtype=np.float32), "eye_f").ap()
    eye_b_c = nc.inline_tensor(np.eye(128, dtype=ml_dtypes.bfloat16), "eye_b").ap()
    triu_c = nc.inline_tensor(np.triu(np.ones((128, 128), np.float32)), "triu").ap()
    triu4_c = nc.inline_tensor(np.tile(np.triu(np.ones((128, 128), np.float32)), (1, 4)),
                               "triu4").ap()
    zeros_c = nc.inline_tensor(np.zeros((128, 512), np.float32), "zeros512").ap()
    rk0 = np.zeros((128, NT * 9), np.float32)
    for j in range(NT):
        for e in range(8):
            rk0[:, 9 * j + e] = e
    rank0_c = nc.inline_tensor(rk0, "rank0").ap()

    sT = Sem(es, nc, "sT")
    sV = Sem(es, nc, "sV", rec=R, eng="v")
    sS = Sem(es, nc, "sS", rec=R, eng="s")
    cC = Sem(es, nc, "cC"); cRK = Sem(es, nc, "cRK")
    cX = [Sem(es, nc, "cX0"), Sem(es, nc, "cX1")]
    cO1 = [Sem(es, nc, "cO1a"), Sem(es, nc, "cO1b")]
    cOG = [Sem(es, nc, "cOGa"), Sem(es, nc, "cOGb")]
    cOS = [Sem(es, nc, "cOSa"), Sem(es, nc, "cOSb")]
    cXW = [Sem(es, nc, "cXWa"), Sem(es, nc, "cXWb")]
    cH = [Sem(es, nc, "cHa"), Sem(es, nc, "cHb")]
    cHL = [Sem(es, nc, "cHLa"), Sem(es, nc, "cHLb")]
    cACC = Sem(es, nc, "cACC"); cY = Sem(es, nc, "cY")
    cRT = Sem(es, nc, "cRT")
    cWR = [Sem(es, nc, f"cWR{i}") for i in range(5)]
    cGK = Sem(es, nc, "cGK")
    cGU = [Sem(es, nc, f"cGU{i}") for i in range(4)]
    cWD = [Sem(es, nc, "cWD0"), Sem(es, nc, "cWD1")]
    cWO = Sem(es, nc, "cWO")
    DMASEMS = ([cC, cRK, cACC, cY, cRT, cGK, cWO] + cX + cO1 + cOG + cOS
               + cXW + cH + cHL + cWR + cGU + cWD)

    def sbl(name, shape, dt):
        return es.enter_context(nc.sbuf_tensor(name, list(shape), dt, side="left"))

    eye_fs = sbl("eye_fs", (128, 128), F32)
    eye_bs = sbl("eye_bs", (128, 128), BF16)
    triu4_s = sbl("triu4_s", (128, 512), F32)
    zeros_s = sbl("zeros_s", (128, 512), F32)
    cols = sbl("cols", (128, 8), F32)
    gnorm_c = sbl("gnorm_c", (128, 1), F32)
    C_nat = sbl("C_nat", (128, NT * 8), F32)
    sgate_n = sbl("sgate_n", (128, NT), F32)
    S_in = sbl("S_in", (128, H * DH), F32)
    z1b = sbl("z1b", (32, T), BF16)
    wgk1_b = sbl("wgk1_b", (128, KB * 16), BF16)
    wgk2_b = sbl("wgk2_b", (32, DH), BF16)
    wr9_hi = sbl("wr9_hi", (128, KB * 9), BF16)
    wr9_lo = sbl("wr9_lo", (128, KB * 9), BF16)
    stat = sbl("stat", (128, 8), F32)
    xnT_es = ExitStack()
    xnT = xnT_es.enter_context(nc.sbuf_tensor("xnT", [128, KB * T], BF16, side="left"))

    CZ, CE2048, CE128, CONE = 0, 1, 2, 3

    R.op("y", "dma_start", eye_fs[:, :], eye_f_c).then_inc(cC.bump(16), 16)
    R.op("y", "dma_start", eye_bs[:, :], eye_b_c).then_inc(cC.bump(16), 16)
    R.op("y", "dma_start", triu4_s[:, :], triu4_c).then_inc(cC.bump(16), 16)
    R.op("y", "dma_start", zeros_s[:, :], zeros_c).then_inc(cC.bump(16), 16)
    R.op("y", "dma_start", cols[:, CZ:CZ + 1], zcol_h).then_inc(cC.bump(16), 16)
    R.op("y", "dma_start", gnorm_c[:, 0:1],
         gnw_h.rearrange("(p o) -> p o", o=1)).then_inc(cC.bump(16), 16)
    consts_ld = cC.n
    R.op("v", "memset", cols[:, CE2048:CE2048 + 1], 2048.0 * EPS)
    R.op("v", "memset", cols[:, CE128:CE128 + 1], 128.0 * EPS)
    R.op("v", "memset", cols[:, CONE:CONE + 1], 1.0)
    R.op("v", "wait_ge", cC.h, consts_ld)
    R.op("v", "tensor_scalar_mul", gnorm_c[:, 0:1], gnorm_c[:, 0:1], SQ128)
    R.mark("v", sV)

    rs0 = ExitStack()
    wr9_f = rs0.enter_context(nc.sbuf_tensor("wr9_f", [128, KB * 9], F32, side="right"))
    wrf = rs0.enter_context(nc.sbuf_tensor("wrf", [128, KB * 8], F32, side="right"))
    wsgT = rs0.enter_context(nc.sbuf_tensor("wsgT", [16, 128], F32, side="right"))
    R.op("g", "dma_start", wrf[:, :].rearrange("p (kb e) -> p kb e", kb=KB),
         wr_h.rearrange("(kb p) e -> p kb e", p=128)).then_inc(cRT.bump(16), 16)
    R.op("g", "dma_start", wsgT[:, :],
         wsg_h.rearrange("(kb p) o -> kb (p o)", p=128)).then_inc(cRT.bump(16), 16)
    R.op("g", "dma_start", wgk1_b[:, :].rearrange("p (kb e) -> p kb e", kb=KB),
         wgk1_h.rearrange("(kb p) e -> p kb e", p=128)).then_inc(cRT.bump(16), 16)
    wr9_ld = cRT.n

    phase_done_cb = [None]

    def phase_done(name):
        if phase_limit == name:
            raise _Done()
    phase_done_cb[0] = phase_done

    def barrier():
        R.mark("t", sT)
        R.mark("v", sV)
        R.mark("s", sS)
        waits = [(sT.h, sT.n), (sV.h, sV.n), (sS.h, sS.n)] + [
            (d.h, d.n) for d in DMASEMS if d.n > 0]
        for e in ENGS:
            for (hh, vv) in waits:
                if vv > 0:
                    R.op(e, "wait_ge", hh, vv)

    rs1 = ExitStack()

    def sbr1(name, shape, dt):
        return rs1.enter_context(nc.sbuf_tensor(name, list(shape), dt, side="right"))

    wring = [sbr1(f"wring{i}", (128, KB * 128), BF16) for i in range(4)]
    gku = sbr1("gku", (128, T), F32)
    Gg = sbr1("Gg", (128, T), F32)
    tmpA = sbr1("tmpA", (128, 512), F32)
    tmpB = sbr1("tmpB", (128, 512), F32)
    tmpC = sbr1("tmpC", (128, 512), F32)
    kT = sbr1("kT", (128, T), BF16)
    vnat = sbr1("vnat", (128, T), BF16)
    tmpv = sbr1("tmpv", (128, 512), BF16)
    ccol = sbr1("ccol", (128, 16 * 8), F32)
    CEND, CPRV, CB1, CB2, CC1, CC2, CDEC, CTMP = range(8)
    E2all = sbr1("E2all", (128, T), F32)
    kwall = sbr1("kwall", (128, T), BF16)
    kwnall = sbr1("kwnall", (128, T), BF16)
    Sf = sbr1("Sf", (128, 128), F32)
    tr128 = sbr1("tr128", (128, 128), BF16)
    sscol = sbr1("sscol", (128, 16), F32)
    rstdall = sbr1("rstdall", (128, 16), F32)
    xnf = gku
    sqtrash = Gg
    rsLN = ExitStack()
    xt = [rsLN.enter_context(nc.sbuf_tensor("xt0", [128, D], F32, side="right")),
          rsLN.enter_context(nc.sbuf_tensor("xt1", [128, D], F32, side="right"))]
    loT = rsLN.enter_context(nc.sbuf_tensor("loT", [128, 512], BF16, side="right"))

    G = {
        "proj_v": [0, 0], "proj_s": [0, 0], "vt": 0, "vt_t": 0, "scf": 0, "scb": 0,
        "wring_tok": [0] * 4, "wslot": 0, "wgk2_tok": 0, "wgk2_ld": 0,
        "pstr_s": [0, 0], "pstr_v": [0, 0], "pstr_t": [0, 0],
        "xnf_t": 0, "lo_t": 0, "head_s_guard": 0, "chunk_pe": 0,
        "ebuf_v": 0, "pkw_pe": 0, "pkw_v": 0,
    }
    PS = {}

    def ln_phase(x_src, is_seg):
        first_bank = [True, True, True, True]
        for i in range(NT):
            buf = i % 2
            if G[f"xt_guard{buf}"]:
                R.op("y", "wait_ge", sV.h, G[f"xt_guard{buf}"])
            R.op("y", "dma_start", xt[buf][:, :],
                 x_src[i * 128:(i + 1) * 128, :]).then_inc(cX[buf].bump(16), 16)
            x_ld = cX[buf].n
            R.op("v", "wait_ge", cX[buf].h, x_ld)
            R.op("v", "tensor_scalar", sqtrash[:, :], xt[buf][:, :], 1.0, 0.0, ALU.mult,
                 ALU.add, accum_out=stat[:, 0:1])
            R.op("v", "tensor_scalar_mul", stat[:, 1:2], stat[:, 0:1], -1.0 / 2048.0)
            R.mark("v", sV)
            v_nm = sV.n
            R.op("s", "wait_ge", sV.h, v_nm)
            R.op("s", "activation", sqtrash[:, :], xt[buf][:, :], AF.Square,
                 bias=stat[:, 1:2], accum_out=stat[:, 2:3])
            R.op("s", "activation", stat[:, 3:4], stat[:, 2:3], AF.Ln,
                 bias=cols[:, CE2048:CE2048 + 1])
            R.op("s", "activation", stat[:, 4:5], stat[:, 3:4], AF.Exp, scale=-0.5)
            R.mark("s", sS)
            s_r = sS.n
            R.op("v", "wait_ge", sS.h, s_r)
            R.op("v", "tensor_scalar_mul", stat[:, 5:6], stat[:, 4:5], SQ2048)
            if G["xnf_t"]:
                R.op("v", "wait_ge", sT.h, G["xnf_t"])
            R.op("v", "tensor_scalar", xnf[:, :], xt[buf][:, :], stat[:, 1:2],
                 stat[:, 5:6], ALU.add, ALU.mult)
            R.mark("v", sV)
            v_xn = sV.n
            G[f"xt_guard{buf}"] = v_xn
            for grp in range(4):
                pb = grp % 2
                R.op("t", "wait_ge", sV.h, v_xn)
                if G["pstr_s"][pb]:
                    R.op("t", "wait_ge", sS.h, G["pstr_s"][pb])
                if G["pstr_v"][pb]:
                    R.op("t", "wait_ge", sV.h, G["pstr_v"][pb])
                for q in range(4):
                    kb = grp * 4 + q
                    R.op("t", "transpose", PS["tr"][pb][:, q * 128:(q + 1) * 128],
                         xnf[:, kb * 128:(kb + 1) * 128], eye_fs[:, :])
                R.mark("t", sT)
                t_tr = sT.n
                if grp == 3:
                    G["xnf_t"] = t_tr
                R.op("s", "wait_ge", sT.h, t_tr)
                for q in range(4):
                    kb = grp * 4 + q
                    R.op("s", "copy", xnT[:, kb * T + i * 128: kb * T + (i + 1) * 128],
                         PS["tr"][pb][:, q * 128:(q + 1) * 128])
                R.mark("s", sS)
                G["pstr_s"][pb] = sS.n
                if is_seg:
                    R.op("v", "wait_ge", sS.h, sS.n)
                    if G["lo_t"]:
                        R.op("v", "wait_ge", sT.h, G["lo_t"])
                    for q in range(4):
                        kb = grp * 4 + q
                        R.op("v", "tensor_tensor", loT[:, q * 128:(q + 1) * 128],
                             PS["tr"][pb][:, q * 128:(q + 1) * 128],
                             xnT[:, kb * T + i * 128: kb * T + (i + 1) * 128], ALU.subtract)
                    R.mark("v", sV)
                    v_lo = sV.n
                    G["pstr_v"][pb] = v_lo
                    bank_i = i // 4
                    R.op("t", "wait_ge", sV.h, v_lo)
                    for q in range(4):
                        kb = grp * 4 + q
                        st = first_bank[bank_i]
                        first_bank[bank_i] = False
                        hi_ap = xnT[:, kb * T + i * 128: kb * T + (i + 1) * 128]
                        last = (i % 4 == 3) and (kb == KB - 1)
                        R.op("t", "matmul", PS["lg"][0:9, i * 128:(i + 1) * 128],
                             wr9_hi[:, kb * 9:(kb + 1) * 9], hi_ap, start=st, stop=False)
                        R.op("t", "matmul", PS["lg"][0:9, i * 128:(i + 1) * 128],
                             wr9_lo[:, kb * 9:(kb + 1) * 9], hi_ap, start=False, stop=False)
                        R.op("t", "matmul", PS["lg"][0:9, i * 128:(i + 1) * 128],
                             wr9_hi[:, kb * 9:(kb + 1) * 9], loT[:, q * 128:(q + 1) * 128],
                             start=False, stop=last)
                    R.mark("t", sT)
                    G["pstr_t"][pb] = sT.n
                    G["lo_t"] = sT.n

    G["xt_guard0"] = G["xt_guard1"] = 0

    def compute_z1():
        R.op("v", "memset", z1b[0:32, :], 1.0)
        R.mark("v", sV)
        v_ones = sV.n
        R.op("t", "wait_ge", cRT.h, wr9_ld)
        for kb in range(KB):
            for n in range(4):
                R.op("t", "matmul", PS["lg"][0:16, n * 512:(n + 1) * 512],
                     wgk1_b[:, kb * 16:(kb + 1) * 16],
                     xnT[:, kb * T + n * 512: kb * T + (n + 1) * 512],
                     start=(kb == 0), stop=(kb == KB - 1))
        R.mark("t", sT)
        t_z1 = sT.n
        R.op("v", "wait_ge", sT.h, t_z1)
        R.op("v", "wait_ge", sV.h, v_ones)
        R.op("v", "tensor_copy", z1b[0:16, :], PS["lg"][0:16, :])
        R.mark("v", sV)
        return sV.n

    def load_whead(col_src):
        sl = G["wslot"] % 4
        G["wslot"] += 1
        if G["wring_tok"][sl]:
            R.op("g", "wait_ge", sT.h, G["wring_tok"][sl])
        R.op("g", "dma_start", wring[sl][:, :].rearrange("p (kb m) -> p kb m", kb=KB),
             col_src).then_inc(cWR[sl].bump(16), 16)
        return sl, cWR[sl].n

    def load_wgk2(h):
        if G["wgk2_tok"]:
            R.op("g", "wait_ge", sT.h, G["wgk2_tok"])
        R.op("g", "dma_start", wgk2_b[0:16, :],
             wgk2_h[:, h * DH:(h + 1) * DH]).then_inc(cGK.bump(16), 16)
        R.op("g", "dma_start", wgk2_b[16:17, :],
             bgk2_h.rearrange("(o d) -> o d", o=1)[:, h * DH:(h + 1) * DH]).then_inc(cGK.bump(16), 16)
        G["wgk2_ld"] = cGK.n

    def proj_wait(pb):
        if G["proj_v"][pb]:
            R.op("t", "wait_ge", sV.h, G["proj_v"][pb])
        if G["proj_s"][pb]:
            R.op("t", "wait_ge", sS.h, G["proj_s"][pb])

    def mm_proj(slot, w_ld, out_T, mask=False):
        R.op("t", "wait_ge", cWR[slot].h, w_ld)
        for q in range(4):
            pb = q % 2
            proj_wait(pb)
            for kb in range(KB):
                off = kb * T + q * 512
                R.op("t", "matmul", PS["proj"][pb][:, :],
                     wring[slot][:, kb * 128:(kb + 1) * 128],
                     xnT[:, off:off + 512], start=(kb == 0), stop=(kb == KB - 1))
            t_mm = R.mark("t", sT)
            G["wring_tok"][slot] = t_mm
            R.op("v", "wait_ge", sT.h, t_mm)
            if mask:
                R.op("v", "tensor_scalar_mul", out_T[:, q * 512:(q + 1) * 512],
                     PS["proj"][pb][:, :], cols[:, CZ:CZ + 1])
            else:
                R.op("v", "tensor_copy", out_T[:, q * 512:(q + 1) * 512],
                     PS["proj"][pb][:, :])
            G["proj_v"][pb] = R.mark("v", sV)

    def mm_proj_v(slot, w_ld):
        R.op("t", "wait_ge", cWR[slot].h, w_ld)
        for q in range(4):
            pb = q % 2
            sb = q % 2
            proj_wait(pb)
            for kb in range(KB):
                off = kb * T + q * 512
                R.op("t", "matmul", PS["proj"][pb][:, :],
                     wring[slot][:, kb * 128:(kb + 1) * 128],
                     xnT[:, off:off + 512], start=(kb == 0), stop=(kb == KB - 1))
            t_mm = R.mark("t", sT)
            G["wring_tok"][slot] = t_mm
            R.op("v", "wait_ge", sT.h, t_mm)
            if G["vt_t"]:
                R.op("v", "wait_ge", sT.h, G["vt_t"])
            R.op("v", "tensor_copy", tmpv[:, :], PS["proj"][pb][:, :])
            v_tm = R.mark("v", sV)
            G["proj_v"][pb] = v_tm
            R.op("t", "wait_ge", sV.h, v_tm)
            if G["scb_g"][sb]:
                R.op("t", "wait_ge", sV.h, G["scb_g"][sb])
            for qq in range(4):
                R.op("t", "transpose", PS["scb"][sb][:, qq * 128:(qq + 1) * 128],
                     tmpv[:, qq * 128:(qq + 1) * 128], eye_bs[:, :])
            t_vt = R.mark("t", sT)
            G["vt_t"] = t_vt
            R.op("v", "wait_ge", sT.h, t_vt)
            R.op("v", "tensor_copy", vnat[:, q * 512:(q + 1) * 512], PS["scb"][sb][:, :])
            G["scb_g"][sb] = R.mark("v", sV)

    def mm_proj_gate(slot, w_ld):
        R.op("t", "wait_ge", cWR[slot].h, w_ld)
        for q in range(4):
            pb = q % 2
            proj_wait(pb)
            for kb in range(KB):
                off = kb * T + q * 512
                R.op("t", "matmul", PS["proj"][pb][:, :],
                     wring[slot][:, kb * 128:(kb + 1) * 128],
                     xnT[:, off:off + 512], start=(kb == 0), stop=(kb == KB - 1))
            t_mm = R.mark("t", sT)
            G["wring_tok"][slot] = t_mm
            R.op("s", "wait_ge", sT.h, t_mm)
            if G.get("sgT_guard"):
                R.op("s", "wait_ge", sV.h, G["sgT_guard"])
            R.op("s", "activation", sgT[:, q * 512:(q + 1) * 512],
                 PS["proj"][pb][:, :], AF.Silu)
            G["proj_s"][pb] = R.mark("s", sS)

    def mm_z2_gk(v_z1):
        R.op("t", "wait_ge", sV.h, v_z1)
        R.op("t", "wait_ge", cGK.h, G["wgk2_ld"])
        for q in range(4):
            pb = q % 2
            proj_wait(pb)
            qoff = q * 512
            R.op("t", "matmul", PS["proj"][pb][:, :],
                 wgk2_b[0:17, :], z1b[0:17, qoff:qoff + 512], start=True, stop=True)
            t_z2 = R.mark("t", sT)
            G["wgk2_tok"] = t_z2
            zsrc = PS["proj"][pb][:, :]
            R.op("s", "wait_ge", sT.h, t_z2)
            if q:
                R.op("s", "wait_ge", sV.h, sV.n)
            R.op("s", "activation", tmpA[:, :], zsrc, AF.Abs)
            s_abs = R.mark("s", sS)
            R.op("s", "activation", tmpB[:, :], tmpA[:, :], AF.Exp, scale=-1.0)
            R.op("s", "activation", tmpA[:, :], tmpB[:, :], AF.Ln,
                 bias=cols[:, CONE:CONE + 1])
            s_ln = R.mark("s", sS)
            G["proj_s"][pb] = s_ln
            R.op("v", "wait_ge", sS.h, s_abs)
            R.op("v", "tensor_scalar_min", tmpC[:, :], zsrc, 0.0)
            R.op("v", "wait_ge", sS.h, s_ln)
            R.op("v", "tensor_tensor", gku[:, qoff:qoff + 512], tmpC[:, :],
                 tmpA[:, :], ALU.subtract)
            G["proj_v"][pb] = R.mark("v", sV)
        phase_done_cb[0]("g1")
        if G["head_s_guard"]:
            R.op("v", "wait_ge", sS.h, G["head_s_guard"])
        for s4 in range(4):
            init = 0.0 if s4 == 0 else Gg[:, s4 * 512 - 1: s4 * 512]
            R.op("v", "tensor_tensor_scan", Gg[:, s4 * 512:(s4 + 1) * 512],
                 gku[:, s4 * 512:(s4 + 1) * 512], zeros_s[:, :], init, ALU.add, ALU.add)
        phase_done_cb[0]("g2")
        R.op("v", "tensor_copy", ccol[:, CEND * 16:CEND * 16 + 16], Gg[:, 127::128])
        R.op("v", "memset", ccol[:, CPRV * 16:CPRV * 16 + 1], 0.0)
        R.op("v", "tensor_copy", ccol[:, CPRV * 16 + 1:CPRV * 16 + 16],
             Gg[:, 127:T - 128:128])
        R.mark("v", sV)
        phase_done_cb[0]("g3")
        return sV.n

    try:
        psA = ExitStack()
        PS["tr"] = [psA.enter_context(nc.psum_tensor("ps_tr0", [128, 512], F32)),
                    psA.enter_context(nc.psum_tensor("ps_tr1", [128, 512], F32))]
        PS["lg"] = psA.enter_context(nc.psum_tensor("ps_lg", [16, 2048], F32))
        R.op("t", "wait_ge", cRT.h, wr9_ld)
        R.op("t", "wait_ge", cC.h, consts_ld)
        R.op("t", "transpose", PS["tr"][0][:, 0:16], wsgT[:, :], eye_fs[0:16, 0:16])
        t_wsg = R.mark("t", sT)
        R.op("v", "wait_ge", sT.h, t_wsg)
        R.op("v", "wait_ge", cRT.h, wr9_ld)
        R.op("v", "tensor_copy",
             wr9_f[:, :].rearrange("p (kb e) -> p kb e", kb=KB)[:, :, 0:8],
             wrf[:, :].rearrange("p (kb e) -> p kb e", kb=KB))
        R.op("v", "tensor_copy",
             wr9_f[:, :].rearrange("p (kb e) -> p kb e", kb=KB)[:, :, 8:9],
             PS["tr"][0][:, 0:16].rearrange("p (kb o) -> p kb o", o=1))
        R.op("v", "tensor_copy", wr9_hi[:, :], wr9_f[:, :])
        R.op("v", "tensor_tensor", wr9_lo[:, :], wr9_f[:, :], wr9_hi[:, :], ALU.subtract)
        v_wr9 = R.mark("v", sV)
        G["pstr_v"][0] = v_wr9
        ln_phase(xpre, False)
        barrier()
        v_z1 = compute_z1()
        barrier()
        phase_done("lnpre")
        psA.close()

        psB = ExitStack()
        PS["proj"] = [psB.enter_context(nc.psum_tensor("ps_pA", [128, 512], F32)),
                      psB.enter_context(nc.psum_tensor("ps_pB", [128, 512], F32))]
        PS["su"] = [psB.enter_context(nc.psum_tensor("ps_suA", [128, 512], F32)),
                    psB.enter_context(nc.psum_tensor("ps_suB", [128, 512], F32))]
        PS["scb"] = [psB.enter_context(nc.psum_tensor("ps_scb0", [128, 512], BF16)),
                     psB.enter_context(nc.psum_tensor("ps_scb1", [128, 512], BF16))]
        G["scb_g"] = [0, 0]
        G["su_g"] = [0, 0]

        def su_chain(seg_mode):
            v_kw = G["pkw_v"]
            kwn_marks = []
            for rnd in range(4):
                sb = rnd % 2
                R.op("t", "wait_ge", sV.h, v_kw)
                if G["scb_g"][sb]:
                    R.op("t", "wait_ge", sV.h, G["scb_g"][sb])
                for jj in range(4):
                    j = rnd * 4 + jj
                    R.op("t", "transpose", PS["scb"][sb][:, jj * 128:(jj + 1) * 128],
                         kwall[:, j * 128:(j + 1) * 128], eye_bs[:, :])
                t_kwt = R.mark("t", sT)
                R.op("v", "wait_ge", sT.h, t_kwt)
                R.op("v", "tensor_copy", kwnall[:, rnd * 512:(rnd + 1) * 512],
                     PS["scb"][sb][:, :])
                G["scb_g"][sb] = R.mark("v", sV)
                kwn_marks.append(G["scb_g"][sb])
            chainmark = []
            if seg_mode:
                R.op("v", "wait_ge", sS.h, G["s_cols"])
                R.op("v", "tensor_copy", Sf[:, :], S_in[:, G["h"] * DH:(G["h"] + 1) * DH])
                R.op("v", "tensor_copy", SbfAll[:, 0:128], Sf[:, :])
                chainmark.append(R.mark("v", sV))
            else:
                R.op("v", "wait_ge", sS.h, G["s_cols"])
                R.op("v", "memset", Sf[:, :], 0.0)
                chainmark.append(R.mark("v", sV))
            nch_chain = NCH - 1 if seg_mode else NCH
            for burst in range(4):
                pb = burst % 2
                R.op("t", "wait_ge", sV.h, kwn_marks[burst])
                if G["su_g"][pb]:
                    R.op("t", "wait_ge", sV.h, G["su_g"][pb])
                for jj in range(4):
                    j = burst * 4 + jj
                    R.op("t", "matmul", PS["su"][pb][:, jj * 128:(jj + 1) * 128],
                         kwnall[:, j * 128:(j + 1) * 128], vnat[:, j * 128:(j + 1) * 128],
                         start=True, stop=True)
                t_su = R.mark("t", sT)
                R.op("v", "wait_ge", sT.h, t_su)
                for jj in range(4):
                    j = burst * 4 + jj
                    if j >= nch_chain:
                        break
                    R.op("v", "scalar_tensor_tensor", Sf[:, :], Sf[:, :],
                         ccol[:, CDEC * 16 + j:CDEC * 16 + j + 1],
                         PS["su"][pb][:, jj * 128:(jj + 1) * 128],
                         ALU.mult, ALU.add)
                    if seg_mode:
                        R.op("v", "tensor_copy",
                             SbfAll[:, (j + 1) * 128:(j + 2) * 128], Sf[:, :])
                    chainmark.append(R.mark("v", sV))
                G["su_g"][pb] = chainmark[-1]
            if not seg_mode:
                R.op("v", "tensor_copy", S_in[:, G["h"] * DH:(G["h"] + 1) * DH], Sf[:, :])
                R.mark("v", sV)
            return chainmark

        for h in range(H):
            G["h"] = h
            load_wgk2(h)
            slk, ldk = load_whead(wk_h.rearrange("(kb p) m -> p kb m", p=128)[:, :, h * DH:(h + 1) * DH])
            slv, ldv = load_whead(wv_h.rearrange("(kb p) m -> p kb m", p=128)[:, :, h * DH:(h + 1) * DH])
            v_gk = mm_z2_gk(v_z1)
            mm_proj(slk, ldk, kT, mask=True)
            mm_proj_v(slv, ldv)
            R.op("v", "tensor_scalar_mul", ccol[:, CB2 * 16:CB2 * 16 + 16],
                 ccol[:, CEND * 16:CEND * 16 + 16], 1.0 / 16.0)
            R.op("v", "tensor_tensor", ccol[:, CTMP * 16:CTMP * 16 + 16],
                 ccol[:, CEND * 16:CEND * 16 + 16], ccol[:, CPRV * 16:CPRV * 16 + 16],
                 ALU.subtract)
            v_cols = R.mark("v", sV)
            R.op("s", "wait_ge", sV.h, v_cols)
            R.op("s", "activation", ccol[:, CDEC * 16:CDEC * 16 + 16],
                 ccol[:, CTMP * 16:CTMP * 16 + 16], AF.Exp, scale=1.0 / 16.0)
            G["s_cols"] = R.mark("s", sS)
            if G["pkw_v"]:
                R.op("s", "wait_ge", sV.h, G["pkw_v"])
            for j in range(NCH):
                R.op("s", "activation", E2all[:, j * 128:(j + 1) * 128],
                     Gg[:, j * 128:(j + 1) * 128], AF.Exp,
                     scale=-1.0 / 16.0, bias=ccol[:, CB2 * 16 + j:CB2 * 16 + j + 1])
            s_e4 = R.mark("s", sS)
            G["head_s_guard"] = s_e4
            R.op("v", "wait_ge", sS.h, s_e4)
            if G["pkw_pe"]:
                R.op("v", "wait_ge", sT.h, G["pkw_pe"])
            R.op("v", "tensor_tensor", kwall[:, :], kT[:, :], E2all[:, :], ALU.mult)
            G["pkw_v"] = R.mark("v", sV)
            su_chain(seg_mode=False)
            G["pkw_pe"] = sT.n
            if h == 0:
                barrier()
                phase_done("p_head0")

        barrier()
        phase_done("prefix")
        psB.close()

        psA2 = ExitStack()
        PS["tr"] = [psA2.enter_context(nc.psum_tensor("ps_tr0b", [128, 512], F32)),
                    psA2.enter_context(nc.psum_tensor("ps_tr1b", [128, 512], F32))]
        PS["lg"] = psA2.enter_context(nc.psum_tensor("ps_lgb", [16, 2048], F32))
        G["pstr_s"] = [0, 0]
        G["pstr_v"] = [0, 0]
        ln_phase(xseg, True)
        barrier()
        phase_done("lnseg")

        rsr = ExitStack()

        def sbrr(name, shape, dt):
            return rsr.enter_context(nc.sbuf_tensor(name, list(shape), dt, side="right"))

        lgT = sbrr("lgT", (16, NT * 128), F32)
        lg_n = sbrr("lg_n", (128, NT * 9), F32)
        es_n = sbrr("es_n", (128, NT * 9), F32)
        rank_n = sbrr("rank_n", (128, NT * 9), F32)
        rk_init = sbrr("rk_init", (128, NT * 9), F32)
        s1_n = sbrr("s1_n", (128, NT), F32)
        m1v = sbrr("m1v", (128, NT), F32)

        R.op("y", "dma_start", rk_init[:, :], rank0_c).then_inc(cRK.bump(16), 16)
        rk_ld = cRK.n
        R.op("v", "memset", lgT[0:16, :], 0.0)
        R.op("v", "tensor_copy", lgT[0:9, :], PS["lg"][0:9, :])
        R.mark("v", sV)
        v_lgT = sV.n
        for i in range(NT):
            R.op("t", "wait_ge", sV.h, v_lgT)
            if i:
                R.op("t", "wait_ge", sV.h, sV.n)
            R.op("t", "transpose", PS["tr"][0][:, 0:16], lgT[0:16, i * 128:(i + 1) * 128],
                 eye_fs[0:16, 0:16])
            R.mark("t", sT)
            R.op("v", "wait_ge", sT.h, sT.n)
            R.op("v", "tensor_copy", lg_n[:, i * 9:(i + 1) * 9], PS["tr"][0][:, 0:9])
            R.mark("v", sV)
        R.op("v", "tensor_copy", m1v[:, :], lg_n[:, 0::9])
        for e in range(1, 8):
            R.op("v", "tensor_tensor", m1v[:, :], m1v[:, :], lg_n[:, e::9], ALU.max)
        for i in range(NT):
            R.op("v", "tensor_scalar_sub", es_n[:, i * 9:i * 9 + 9],
                 lg_n[:, i * 9:i * 9 + 9], m1v[:, i:i + 1])
        R.mark("v", sV)
        v_sub = sV.n
        R.op("s", "wait_ge", sV.h, v_sub)
        R.op("s", "activation", es_n[:, :], es_n[:, :], AF.Exp)
        R.op("s", "activation", sgate_n[:, :], lg_n[:, 8::9], AF.Exp, scale=-1.0)
        R.mark("s", sS)
        s_es = sS.n
        R.op("v", "wait_ge", sS.h, s_es)
        R.op("v", "tensor_scalar_add", sgate_n[:, :], sgate_n[:, :], 1.0)
        R.op("v", "reciprocal", sgate_n[:, :], sgate_n[:, :])
        R.op("v", "tensor_copy", s1_n[:, :], es_n[:, 0::9])
        for e in range(1, 8):
            R.op("v", "tensor_tensor", s1_n[:, :], s1_n[:, :], es_n[:, e::9], ALU.add)
        R.op("v", "wait_ge", cRK.h, rk_ld)
        R.op("v", "tensor_copy", rank_n[:, :], rk_init[:, :])
        for a in range(8):
            for b in range(a + 1, 8):
                R.op("v", "tensor_tensor", m1v[:, :], es_n[:, b::9], es_n[:, a::9], ALU.is_gt)
                R.op("v", "tensor_tensor", rank_n[:, a::9], rank_n[:, a::9], m1v[:, :], ALU.add)
                R.op("v", "tensor_tensor", rank_n[:, b::9], rank_n[:, b::9], m1v[:, :],
                     ALU.subtract)
        R.op("v", "tensor_single_scalar", rank_n[:, :], rank_n[:, :], 2.0, ALU.is_lt)
        R.op("v", "tensor_tensor", es_n[:, :], es_n[:, :], rank_n[:, :], ALU.mult)
        R.op("v", "reciprocal", s1_n[:, :], s1_n[:, :])
        for e in range(8):
            R.op("v", "tensor_tensor", C_nat[:, e::8], es_n[:, e::9], s1_n[:, :], ALU.mult)
        R.mark("v", sV)
        barrier()
        phase_done("router")
        rsr.close()
        rsLN.close()

        v_z1 = compute_z1()
        barrier()
        psA2.close()

        psB2 = ExitStack()
        PS["proj"] = [psB2.enter_context(nc.psum_tensor("ps_pA2", [128, 512], F32)),
                      psB2.enter_context(nc.psum_tensor("ps_pB2", [128, 512], F32))]
        PS["su"] = [psB2.enter_context(nc.psum_tensor("ps_suA2", [128, 512], F32)),
                    psB2.enter_context(nc.psum_tensor("ps_suB2", [128, 512], F32))]
        PS["at"] = [psB2.enter_context(nc.psum_tensor("ps_at0", [128, 512], F32)),
                    psB2.enter_context(nc.psum_tensor("ps_at1", [128, 512], F32))]
        PS["scb"] = [psB2.enter_context(nc.psum_tensor("ps_scb0b", [128, 512], BF16)),
                     psB2.enter_context(nc.psum_tensor("ps_scb1b", [128, 512], BF16))]
        G["vt"] = 0
        G["scb_g"] = [0, 0]
        G["su_g"] = [0, 0]
        G["at_g"] = [0, 0]
        G["Eread"] = 0
        G["gate_pe"] = 0
        ogh_store = [0]
        rsAT = ExitStack()

        def sbat(name, shape, dt):
            return rsAT.enter_context(nc.sbuf_tensor(name, list(shape), dt, side="right"))

        qT = sbat("qT", (128, T), BF16)
        sgT = sbat("sgT", (128, T), BF16)
        E1all = sbat("E1all", (128, T), F32)
        qgall = sbat("qgall", (128, T), BF16)
        qhall = sbat("qhall", (128, T), BF16)
        khall = sbat("khall", (128, T), BF16)
        atball = sbat("atball", (128, T), BF16)
        SbfAll = sbat("SbfAll", (128, (NCH + 1) * 128), BF16)
        of32p = sbat("of32p", (128, 1024), F32)
        oscp = sbat("oscp", (128, 1024), BF16)
        ogh = sbat("ogh", (128, T), BF16)

        for h in range(H):
            G["h"] = h
            load_wgk2(h)
            slk, ldk = load_whead(wk_h.rearrange("(kb p) m -> p kb m", p=128)[:, :, h * DH:(h + 1) * DH])
            slv, ldv = load_whead(wv_h.rearrange("(kb p) m -> p kb m", p=128)[:, :, h * DH:(h + 1) * DH])
            slq, ldq = load_whead(wq_h.rearrange("(kb p) m -> p kb m", p=128)[:, :, h * DH:(h + 1) * DH])
            slg, ldg = load_whead(wgate_h.rearrange("(kb p) m -> p kb m", p=128)[:, :, h * DH:(h + 1) * DH])
            v_gk = mm_z2_gk(v_z1)
            mm_proj(slk, ldk, kT, mask=False)
            mm_proj_v(slv, ldv)
            R.op("v", "tensor_copy", ccol[:, CTMP * 16:CTMP * 16 + 16], Gg[:, 0::128])
            R.op("v", "tensor_scalar", ccol[:, CB1 * 16:CB1 * 16 + 16],
                 ccol[:, CPRV * 16:CPRV * 16 + 16], -1.0 / 16.0, LNSCALE, ALU.mult, ALU.add)
            R.op("v", "tensor_scalar_mul", ccol[:, CB2 * 16:CB2 * 16 + 16],
                 ccol[:, CTMP * 16:CTMP * 16 + 16], 1.0 / 16.0)
            R.op("v", "tensor_tensor", ccol[:, CC1 * 16:CC1 * 16 + 16],
                 ccol[:, CPRV * 16:CPRV * 16 + 16], ccol[:, CTMP * 16:CTMP * 16 + 16],
                 ALU.subtract)
            R.op("v", "tensor_tensor", ccol[:, CC2 * 16:CC2 * 16 + 16],
                 ccol[:, CEND * 16:CEND * 16 + 16], ccol[:, CTMP * 16:CTMP * 16 + 16],
                 ALU.subtract)
            R.op("v", "tensor_tensor", ccol[:, CDEC * 16:CDEC * 16 + 16],
                 ccol[:, CEND * 16:CEND * 16 + 16], ccol[:, CPRV * 16:CPRV * 16 + 16],
                 ALU.subtract)
            v_cols = R.mark("v", sV)
            R.op("s", "wait_ge", sV.h, v_cols)
            R.op("s", "activation", ccol[:, CC1 * 16:CC1 * 16 + 16],
                 ccol[:, CC1 * 16:CC1 * 16 + 16], AF.Exp, scale=1.0 / 16.0)
            R.op("s", "activation", ccol[:, CC2 * 16:CC2 * 16 + 16],
                 ccol[:, CC2 * 16:CC2 * 16 + 16], AF.Exp, scale=1.0 / 16.0)
            R.op("s", "activation", ccol[:, CDEC * 16:CDEC * 16 + 16],
                 ccol[:, CDEC * 16:CDEC * 16 + 16], AF.Exp, scale=1.0 / 16.0)
            G["s_cols"] = R.mark("s", sS)
            s_cols = G["s_cols"]
            if G["Eread"]:
                R.op("s", "wait_ge", sV.h, G["Eread"])
            for j in range(NCH):
                R.op("s", "activation", E1all[:, j * 128:(j + 1) * 128],
                     Gg[:, j * 128:(j + 1) * 128], AF.Exp,
                     scale=1.0 / 16.0, bias=ccol[:, CB1 * 16 + j:CB1 * 16 + j + 1])
                R.op("s", "activation", E2all[:, j * 128:(j + 1) * 128],
                     Gg[:, j * 128:(j + 1) * 128], AF.Exp,
                     scale=-1.0 / 16.0, bias=ccol[:, CB2 * 16 + j:CB2 * 16 + j + 1])
            s_e = R.mark("s", sS)
            G["head_s_guard"] = s_e
            R.op("v", "wait_ge", sS.h, s_e)
            if G["gate_pe"]:
                R.op("v", "wait_ge", sT.h, G["gate_pe"])
            R.op("v", "tensor_tensor", khall[:, :], kT[:, :], E2all[:, :], ALU.mult)
            for j in range(NCH):
                R.op("v", "tensor_scalar_mul", kwall[:, j * 128:(j + 1) * 128],
                     khall[:, j * 128:(j + 1) * 128], ccol[:, CC2 * 16 + j:CC2 * 16 + j + 1])
            v_kgates = R.mark("v", sV)
            G["pkw_v"] = v_kgates
            chainmark = su_chain(seg_mode=True)
            mm_proj(slq, ldq, qT, mask=False)
            R.op("v", "tensor_tensor", qgall[:, :], qT[:, :], E1all[:, :], ALU.mult)
            for j in range(NCH):
                R.op("v", "tensor_scalar_mul", qhall[:, j * 128:(j + 1) * 128],
                     qgall[:, j * 128:(j + 1) * 128], ccol[:, CC1 * 16 + j:CC1 * 16 + j + 1])
            v_gates = R.mark("v", sV)
            G["Eread"] = v_gates
            mm_proj_gate(slg, ldg)
            for b4 in range(4):
                pb = b4 % 2
                R.op("t", "wait_ge", sV.h, v_gates)
                if G["at_g"][pb]:
                    R.op("t", "wait_ge", sV.h, G["at_g"][pb])
                for jj in range(4):
                    j = b4 * 4 + jj
                    R.op("t", "matmul", PS["at"][pb][:, jj * 128:(jj + 1) * 128],
                         khall[:, j * 128:(j + 1) * 128], qhall[:, j * 128:(j + 1) * 128],
                         start=True, stop=True)
                t_at = R.mark("t", sT)
                R.op("v", "wait_ge", sT.h, t_at)
                R.op("v", "tensor_tensor", atball[:, b4 * 512:(b4 + 1) * 512],
                     PS["at"][pb][:, :], triu4_s[:, :], ALU.mult)
                G["at_g"][pb] = R.mark("v", sV)
            for b4 in range(4):
                pb = b4 % 2
                R.op("t", "wait_ge", sV.h, chainmark[min(b4 * 4 + 3, NCH - 1)])
                if G["at_g"][pb]:
                    R.op("t", "wait_ge", sV.h, G["at_g"][pb])
                for jj in range(4):
                    j = b4 * 4 + jj
                    R.op("t", "matmul", PS["at"][pb][:, jj * 128:(jj + 1) * 128],
                         atball[:, j * 128:(j + 1) * 128], vnat[:, j * 128:(j + 1) * 128],
                         start=True, stop=False)
                    R.op("t", "matmul", PS["at"][pb][:, jj * 128:(jj + 1) * 128],
                         qgall[:, j * 128:(j + 1) * 128], SbfAll[:, j * 128:(j + 1) * 128],
                         start=False, stop=True)
                t_o = R.mark("t", sT)
                R.op("v", "wait_ge", sT.h, t_o)
                R.op("v", "tensor_copy", of32p[:, pb * 512:(pb + 1) * 512], PS["at"][pb][:, :])
                for jj in range(4):
                    j = b4 * 4 + jj
                    R.op("v", "scalar_tensor_tensor", tr128[:, :],
                         of32p[:, pb * 512 + jj * 128: pb * 512 + (jj + 1) * 128], 1.0,
                         of32p[:, pb * 512 + jj * 128: pb * 512 + (jj + 1) * 128],
                         ALU.mult, ALU.mult, accum_out=sscol[:, j:j + 1])
                v_ssq = R.mark("v", sV)
                G["at_g"][pb] = v_ssq
                R.op("s", "wait_ge", sV.h, v_ssq)
                R.op("s", "activation", rstdall[:, b4 * 4:b4 * 4 + 4],
                     sscol[:, b4 * 4:b4 * 4 + 4], AF.Ln, bias=cols[:, CE128:CE128 + 1])
                R.op("s", "activation", rstdall[:, b4 * 4:b4 * 4 + 4],
                     rstdall[:, b4 * 4:b4 * 4 + 4], AF.Exp, scale=-0.5)
                s_rs = R.mark("s", sS)
                R.op("v", "wait_ge", sS.h, s_rs)
                for jj in range(4):
                    j = b4 * 4 + jj
                    R.op("v", "tensor_scalar_mul",
                         oscp[:, pb * 512 + jj * 128: pb * 512 + (jj + 1) * 128],
                         of32p[:, pb * 512 + jj * 128: pb * 512 + (jj + 1) * 128],
                         rstdall[:, j:j + 1])
                v_osc = R.mark("v", sV)
                R.op("t", "wait_ge", sV.h, v_osc)
                if G["scb_g"][pb]:
                    R.op("t", "wait_ge", sV.h, G["scb_g"][pb])
                for jj in range(4):
                    R.op("t", "transpose",
                         PS["scb"][pb][:, jj * 128:(jj + 1) * 128],
                         oscp[:, pb * 512 + jj * 128: pb * 512 + (jj + 1) * 128],
                         eye_bs[:, :])
                t_ot = R.mark("t", sT)
                R.op("v", "wait_ge", sT.h, t_ot)
                if ogh_store[0]:
                    R.op("v", "wait_ge", cOG[0].h, ogh_store[0])
                R.op("v", "scalar_tensor_tensor", ogh[:, b4 * 512:(b4 + 1) * 512],
                     PS["scb"][pb][:, :], gnorm_c[:, 0:1],
                     sgT[:, b4 * 512:(b4 + 1) * 512], ALU.mult, ALU.mult)
                G["scb_g"][pb] = R.mark("v", sV)
            G["gate_pe"] = sT.n
            R.op("y", "wait_ge", sV.h, sV.n)
            R.op("y", "dma_start", og_hbm[h * DH:(h + 1) * DH, :],
                 ogh[:, :]).then_inc(cOG[0].bump(16), 16)
            ogh_store[0] = cOG[0].n

        barrier()
        phase_done("attn")
        psB2.close()
        rsAT.close()
        rs1.close()
        rs0.close()

        psW = ExitStack()
        PS["proj"] = [psW.enter_context(nc.psum_tensor("ps_woA", [128, 1024], F32)),
                      psW.enter_context(nc.psum_tensor("ps_woB", [128, 1024], F32))]
        rsW = ExitStack()

        def sbrw(name, shape, dt):
            return rsW.enter_context(nc.sbuf_tensor(name, list(shape), dt, side="right"))

        wo_b = sbrw("wo_b", (128, KB * D), BF16)
        ogsl = [sbrw("ogsl0", (128, H * 128), BF16), sbrw("ogsl1", (128, H * 128), BF16)]
        xtw = [sbrw("xtw0", (128, D), F32), sbrw("xtw1", (128, D), F32)]
        o1sb = [sbrw("o1sb0", (128, 1024), F32), sbrw("o1sb1", (128, 1024), F32)]
        R.op("g", "dma_start", wo_b[:, :].rearrange("p (kb n) -> p kb n", kb=KB),
             wo_h.rearrange("(kb p) n -> p kb n", p=128)).then_inc(cWO.bump(16), 16)
        wo_ld = cWO.n
        G["proj_v"] = [0, 0]
        G["proj_s"] = [0, 0]
        ogsl_tok = [0, 0]
        xtw_tok = [0, 0]
        o1_store = [0, 0]
        for tb in range(NT):
            buf = tb % 2
            if ogsl_tok[buf]:
                R.op("y", "wait_ge", sT.h, ogsl_tok[buf])
            R.op("y", "dma_start", ogsl[buf][:, :].rearrange("p (hh m) -> p hh m", hh=H),
                 og_hbm[:, tb * 128:(tb + 1) * 128].rearrange("(hh p) m -> p hh m", p=128)
                 ).then_inc(cOS[buf].bump(16), 16)
            og_ld = cOS[buf].n
            if xtw_tok[buf]:
                R.op("y", "wait_ge", sV.h, xtw_tok[buf])
            R.op("y", "dma_start", xtw[buf][:, :],
                 xseg[tb * 128:(tb + 1) * 128, :]).then_inc(cXW[buf].bump(16), 16)
            x_ld = cXW[buf].n
            for half in range(2):
                pb = half
                R.op("t", "wait_ge", cWO.h, wo_ld)
                R.op("t", "wait_ge", cOS[buf].h, og_ld)
                proj_wait(pb)
                for hh in range(H):
                    for n in range(2):
                        woff = hh * D + half * 1024 + n * 512
                        R.op("t", "matmul", PS["proj"][pb][:, n * 512:(n + 1) * 512],
                             ogsl[buf][:, hh * 128:(hh + 1) * 128],
                             wo_b[:, woff:woff + 512], start=(hh == 0), stop=(hh == H - 1))
                R.mark("t", sT)
                t_wo = sT.n
                if half == 1:
                    ogsl_tok[buf] = t_wo
                R.op("v", "wait_ge", sT.h, t_wo)
                R.op("v", "wait_ge", cXW[buf].h, x_ld)
                if o1_store[half]:
                    R.op("v", "wait_ge", cO1[half].h, o1_store[half])
                R.op("v", "tensor_tensor", o1sb[half][:, :], PS["proj"][pb][:, :],
                     xtw[buf][:, half * 1024:(half + 1) * 1024], ALU.add)
                R.mark("v", sV)
                G["proj_v"][pb] = sV.n
                xtw_tok[buf] = sV.n
                R.op("y", "wait_ge", sV.h, sV.n)
                R.op("y", "dma_start",
                     out1_hbm[tb * 128:(tb + 1) * 128, half * 1024:(half + 1) * 1024],
                     o1sb[half][:, :]).then_inc(cO1[half].bump(16), 16)
                o1_store[half] = cO1[half].n

        barrier()
        phase_done("wo")
        psW.close()
        rsW.close()

        psF = ExitStack()
        PS["proj"] = [psF.enter_context(nc.psum_tensor("ps_gA", [128, 1024], F32)),
                      psF.enter_context(nc.psum_tensor("ps_gB", [128, 1024], F32))]
        PS["u"] = [psF.enter_context(nc.psum_tensor("ps_uA", [128, 1024], F32)),
                   psF.enter_context(nc.psum_tensor("ps_uB", [128, 1024], F32))]
        rsF = ExitStack()

        def sbrf(name, shape, dt):
            return rsF.enter_context(nc.sbuf_tensor(name, list(shape), dt, side="right"))

        wgu = [sbrf(f"wgu{i}", (128, KB * 128), BF16) for i in range(4)]
        sgil = sbrf("sgil", (128, 1024), F32)
        hsb = [sbrf("hsb0", (128, T), BF16), sbrf("hsb1", (128, T), BF16)]
        wgu_tok = [0] * 4
        hsb_store = [0, 0]
        G["proj_v"] = [0, 0]
        G["proj_s"] = [0, 0]
        u_guard = [0, 0]
        sgil_guard = 0

        def gu_src(e, fb, which):
            if e < 8:
                t = ewg_h if which == 0 else ewu_h
                return t[e, :, :].rearrange("(kb p) m -> p kb m", p=128)[:, :, fb * 128:(fb + 1) * 128]
            t = sgw_h if which == 0 else suw_h
            off = (e - 8) * F + fb * 128
            return t.rearrange("(kb p) m -> p kb m", p=128)[:, :, off:off + 128]

        idx = 0
        for e in range(NE):
            for fb in range(FB):
                sg_slot = (2 * idx) % 4
                su_slot = (2 * idx + 1) % 4
                if wgu_tok[sg_slot]:
                    R.op("g", "wait_ge", sT.h, wgu_tok[sg_slot])
                R.op("g", "dma_start", wgu[sg_slot][:, :].rearrange("p (kb m) -> p kb m", kb=KB),
                     gu_src(e, fb, 0)).then_inc(cGU[sg_slot].bump(16), 16)
                ld_g = cGU[sg_slot].n
                if wgu_tok[su_slot]:
                    R.op("g", "wait_ge", sT.h, wgu_tok[su_slot])
                R.op("g", "dma_start", wgu[su_slot][:, :].rearrange("p (kb m) -> p kb m", kb=KB),
                     gu_src(e, fb, 1)).then_inc(cGU[su_slot].bump(16), 16)
                ld_u = cGU[su_slot].n
                hb = idx % 2
                for half in range(2):
                    R.op("t", "wait_ge", cGU[sg_slot].h, ld_g)
                    if G["proj_s"][half]:
                        R.op("t", "wait_ge", sS.h, G["proj_s"][half])
                    for kb in range(KB):
                        for n in range(2):
                            off = kb * T + half * 1024 + n * 512
                            R.op("t", "matmul", PS["proj"][half][:, n * 512:(n + 1) * 512],
                                 wgu[sg_slot][:, kb * 128:(kb + 1) * 128],
                                 xnT[:, off:off + 512], start=(kb == 0), stop=(kb == KB - 1))
                    R.mark("t", sT)
                    t_g = sT.n
                    R.op("t", "wait_ge", cGU[su_slot].h, ld_u)
                    if u_guard[half]:
                        R.op("t", "wait_ge", sV.h, u_guard[half])
                    for kb in range(KB):
                        for n in range(2):
                            off = kb * T + half * 1024 + n * 512
                            R.op("t", "matmul", PS["u"][half][:, n * 512:(n + 1) * 512],
                                 wgu[su_slot][:, kb * 128:(kb + 1) * 128],
                                 xnT[:, off:off + 512], start=(kb == 0), stop=(kb == KB - 1))
                    R.mark("t", sT)
                    t_u = sT.n
                    if half == 1:
                        wgu_tok[sg_slot] = t_g
                        wgu_tok[su_slot] = t_u
                    R.op("s", "wait_ge", sT.h, t_g)
                    if sgil_guard:
                        R.op("s", "wait_ge", sV.h, sgil_guard)
                    R.op("s", "activation", sgil[:, :], PS["proj"][half][:, :], AF.Silu)
                    R.mark("s", sS)
                    s_sil = sS.n
                    G["proj_s"][half] = s_sil
                    R.op("v", "wait_ge", sS.h, s_sil)
                    R.op("v", "wait_ge", sT.h, t_u)
                    if hsb_store[hb]:
                        R.op("v", "wait_ge", cH[hb].h, hsb_store[hb])
                    R.op("v", "tensor_tensor", hsb[hb][:, half * 1024:(half + 1) * 1024],
                         PS["u"][half][:, :], sgil[:, :], ALU.mult)
                    R.mark("v", sV)
                    u_guard[half] = sV.n
                    sgil_guard = sV.n
                R.op("y", "wait_ge", sV.h, sV.n)
                R.op("y", "dma_start", h_hbm[e * F + fb * 128: e * F + (fb + 1) * 128, :],
                     hsb[hb][:, :]).then_inc(cH[hb].bump(16), 16)
                hsb_store[hb] = cH[hb].n
                idx += 1

        barrier()
        phase_done("ffna")
        psF.close()
        rsF.close()
        xnT_es.close()

        psM = ExitStack()
        ps_moe = [psM.enter_context(nc.psum_tensor("ps_m0", [128, 512], F32)),
                  psM.enter_context(nc.psum_tensor("ps_m1", [128, 512], F32))]
        rsB = ExitStack()

        def sbrb(name, shape, dt):
            return rsB.enter_context(nc.sbuf_tensor(name, list(shape), dt, side="right"))

        acc = sbrb("acc", (128, 8 * D), F32)
        wd_b = [sbrb("wd0", (128, FB * D), BF16), sbrb("wd1", (128, FB * D), BF16)]
        hsl = [sbrb("hsl0", (128, FB * 128), BF16), sbrb("hsl1", (128, FB * 128), BF16)]

        def wd_src(e):
            if e < 8:
                return ewd_h[e, :, :].rearrange("(fb p) n -> p fb n", p=128)
            return sdw_h[(e - 8) * F:(e - 7) * F, :].rearrange("(fb p) n -> p fb n", p=128)

        wd_tok = [0, 0]
        hsl_tok = [0, 0]
        moe_guard = [0, 0]
        acc_store = 0
        for sub in range(2):
            R.op("y", "wait_ge", sV.h, sV.n)
            if acc_store:
                R.op("y", "wait_ge", cY.h, acc_store)
            R.op("y", "dma_start", acc[:, :].rearrange("p (tb d) -> p tb d", tb=8),
                 out1_hbm[sub * 1024:(sub + 1) * 1024, :].rearrange("(tb p) d -> p tb d", p=128)
                 ).then_inc(cACC.bump(16), 16)
            acc_ld = cACC.n
            for e in range(NE):
                sl = e % 2
                if wd_tok[sl]:
                    R.op("g", "wait_ge", sT.h, wd_tok[sl])
                R.op("g", "dma_start", wd_b[sl][:, :].rearrange("p (fb n) -> p fb n", fb=FB),
                     wd_src(e)).then_inc(cWD[sl].bump(16), 16)
                ld_wd = cWD[sl].n
                for tb8 in range(8):
                    tb = sub * 8 + tb8
                    hb = tb8 % 2
                    if hsl_tok[hb]:
                        R.op("y", "wait_ge", sT.h, hsl_tok[hb])
                    R.op("y", "dma_start", hsl[hb][:, :].rearrange("p (fb m) -> p fb m", fb=FB),
                         h_hbm[e * F:(e + 1) * F, tb * 128:(tb + 1) * 128].rearrange(
                             "(fb p) m -> p fb m", p=128)).then_inc(cHL[hb].bump(16), 16)
                    ld_h = cHL[hb].n
                    for dc in range(4):
                        pb = (tb8 * 4 + dc) % 2
                        R.op("t", "wait_ge", cWD[sl].h, ld_wd)
                        R.op("t", "wait_ge", cHL[hb].h, ld_h)
                        if moe_guard[pb]:
                            R.op("t", "wait_ge", sV.h, moe_guard[pb])
                        for fb in range(FB):
                            R.op("t", "matmul", ps_moe[pb][:, :],
                                 hsl[hb][:, fb * 128:(fb + 1) * 128],
                                 wd_b[sl][:, fb * D + dc * 512: fb * D + (dc + 1) * 512],
                                 start=(fb == 0), stop=(fb == FB - 1))
                        R.mark("t", sT)
                        t_moe = sT.n
                        if dc == 3:
                            hsl_tok[hb] = t_moe
                            if tb8 == 7:
                                wd_tok[sl] = t_moe
                        wc = C_nat[:, tb * 8 + e: tb * 8 + e + 1] if e < 8 else sgate_n[:, tb:tb + 1]
                        R.op("v", "wait_ge", sT.h, t_moe)
                        if e == 0 and dc == 0 and tb8 == 0:
                            R.op("v", "wait_ge", cACC.h, acc_ld)
                        R.op("v", "scalar_tensor_tensor",
                             acc[:, tb8 * D + dc * 512: tb8 * D + (dc + 1) * 512],
                             ps_moe[pb][:, :], wc,
                             acc[:, tb8 * D + dc * 512: tb8 * D + (dc + 1) * 512],
                             ALU.mult, ALU.add)
                        R.mark("v", sV)
                        moe_guard[pb] = sV.n
            R.op("y", "wait_ge", sV.h, sV.n)
            R.op("y", "dma_start",
                 y_out[sub * 1024:(sub + 1) * 1024, :].rearrange("(tb p) d -> p tb d", p=128),
                 acc[:, :].rearrange("p (tb d) -> p tb d", tb=8)).then_inc(cY.bump(16), 16)
            acc_store = cY.n

        R.op("y", "wait_ge", cY.h, cY.n)
    except _Done:
        R.op("y", "dma_start", y_out, xseg).then_inc(cY.bump(16), 16)
        R.op("y", "wait_ge", cY.h, cY.n)

    R.selfsem["v"] = sV.h
    R.selfsem["s"] = sS.h
    with nc.Block() as block:
        @block.tensor
        def _(t):
            R.replay("t", t)

        @block.vector
        def _(v):
            R.replay("v", v)

        @block.scalar
        def _(s):
            R.replay("s", s)

        @block.gpsimd
        def _(g):
            R.replay("g", g)

        @block.sync
        def _(yq):
            R.replay("y", yq)

    if phase_limit == "full":
        psM.close()
        rsB.close()
        es.close()
    return nc


_NC_CACHE = []


def kernel(**inputs):
    x = np.ascontiguousarray(np.asarray(inputs["L0_attn_x"], dtype=np.float32))
    B, L, _ = x.shape
    if not _NC_CACHE:
        _NC_CACHE.append(build_nc())
    nc = _NC_CACHE[0]
    base = {
        "wq": np.ascontiguousarray(inputs["L0_attn_Wq"], np.float32),
        "wk": np.ascontiguousarray(inputs["L0_attn_Wk"], np.float32),
        "wv": np.ascontiguousarray(inputs["L0_attn_Wv"], np.float32),
        "wgk1": np.ascontiguousarray(inputs["L0_attn_Wgk1"], np.float32),
        "wgk2": np.ascontiguousarray(inputs["L0_attn_Wgk2"], np.float32),
        "bgk2": np.ascontiguousarray(inputs["L0_attn_bgk2"], np.float32),
        "wgate": np.ascontiguousarray(inputs["L0_attn_Wg"], np.float32),
        "gnw": np.ascontiguousarray(inputs["L0_attn_gnorm_w"], np.float32),
        "wo": np.ascontiguousarray(inputs["L0_attn_Wo"], np.float32),
        "wr": np.ascontiguousarray(inputs["L0_ffn_Wr"], np.float32),
        "wsg": np.ascontiguousarray(inputs["L0_ffn_Wsg"], np.float32),
        "ewg": np.ascontiguousarray(inputs["L0_ffn_Wg"], np.float32),
        "ewu": np.ascontiguousarray(inputs["L0_ffn_Wu"], np.float32),
        "ewd": np.ascontiguousarray(inputs["L0_ffn_Wd"], np.float32),
        "sgw": np.ascontiguousarray(inputs["L0_ffn_Sg"], np.float32),
        "suw": np.ascontiguousarray(inputs["L0_ffn_Su"], np.float32),
        "sdw": np.ascontiguousarray(inputs["L0_ffn_Sd"], np.float32),
    }
    in_maps = []
    for c in range(8):
        b, s = c // 2, c % 2
        m = dict(base)
        m["xseg"] = np.ascontiguousarray(x[b, s * T:(s + 1) * T, :])
        m["xpre"] = np.ascontiguousarray(x[b, 0:T, :])
        m["zcol"] = np.full((128, 1), 1.0 if s == 1 else 0.0, np.float32)
        in_maps.append(m)
    res = run_bass_kernel_spmd(nc, in_maps, core_ids=list(range(8)))
    out = np.empty((B, L, D), np.float32)
    for c in range(8):
        b, s = c // 2, c % 2
        out[b, s * T:(s + 1) * T, :] = res.results[c]["y"]
    return out
